# version 16
# baseline (speedup 1.0000x reference)
import numpy as np
from contextlib import ExitStack

import concourse.bass as bass
import concourse.mybir as mybir
from concourse.bass_utils import run_bass_kernel_spmd

F32 = mybir.dt.float32
F32R = mybir.dt.float32r
AF = mybir.ActivationFunctionType
ALU = mybir.AluOpType
AXX = mybir.AxisListType.X

NEG_BIG = -1.0e30
LN_EPS = 1e-5


class Cfg:
    def __init__(self, D=2048, H=16, S=4096, INTER=8192, NG=16):
        self.D, self.H, self.S, self.I = D, H, S, INTER
        assert D == H * 128
        self.KD = D // 128
        self.Tc = S // 4
        self.NT = self.Tc // 128
        assert self.NT == S // 512
        self.TM = min(512, self.Tc)
        self.NTM = self.Tc // self.TM
        self.NKT = S // 128
        self.NG = NG
        self.IG = INTER // NG
        self.IGC = self.IG // 128
        assert self.IG * NG == INTER and self.IGC * 128 == self.IG


class Px:

    def __init__(self, eng):
        self._e = eng

    @property
    def live(self):
        return self._e is not None

    def __getattr__(self, name):
        if self._e is None:
            return lambda *a, **k: None
        return getattr(self._e, name)


class Sem:
    def __init__(self, name, handle=None):
        self.name, self.h, self.n = name, handle, 0


class Builder:

    def __init__(self, cfg, tensors, sems, marks):
        self.c = cfg
        self.t = tensors
        self.sems = sems
        self.marks = marks
        self.banks = {}
        self.final = False
        self.hiwater = {}

    def inc(self, inst, semname, k, key=None):
        s = self.sems[semname]
        s.n += k
        if inst is not None:
            inst.then_inc(s.h, k)
        if key is not None:
            if not self.final:
                self.marks[key] = (semname, s.n)
            else:
                assert self.marks[key] == (semname, s.n), (key, self.marks[key], (semname, s.n))
        return key

    def mark_here(self, semname, key):
        if not self.final:
            self.marks[key] = (semname, self.sems[semname].n)
        else:
            assert self.marks[key] == (semname, self.sems[semname].n), key
        return key

    def wait(self, eng, key):
        if key is None:
            return
        if key not in self.marks:
            neg = any(isinstance(x, int) and x < 0 for x in key)
            assert neg or not self.final, f"missing mark {key}"
            return
        semname, v = self.marks[key]
        if v <= self.hiwater.get(semname, 0):
            return
        self.hiwater[semname] = v
        if eng.live:
            eng.wait_ge(self.sems[semname].h, v)

    def bank_use(self, en, eng, bank, evac_key):
        prev = self.banks.get(bank)
        if en == "pe" and prev is not None:
            self.wait(eng, prev)
        self.banks[bank] = evac_key

    def phaseA(self, en, e):
        c, t = self.c, self.t
        atp = [0]
        if en == "sp":
            for nm_in, nm in [("rope_cos_in", "rope_cos"), ("rope_sin_in", "rope_sin"),
                              ("mask_in", "mask"), ("ident_in", "ident")]:
                inst = e.dma_start(out=t[nm], in_=t[nm_in])
                self.inc(inst, "sc", 16)
            self.mark_here("sc", ("consts",))
        for j in range(c.NT):
            xb = t["xbuf"][j % 2]
            if en == "sp":
                self.wait(e, ("xbuf_free", j - 2))
                self.wait(e, ("rb", j - 2))
                self.inc(e.dma_start(out=xb, in_=t["x_own"][j]), f"sx{j % 2}", 16,
                         ("xld", j))
            if en == "dve":
                self.wait(e, ("xld", j))
                self.inc(e.reduce_sum(t["s1"], xb, axis=AXX), "VD", 1, ("ls1", j))
                self.wait(e, ("sq", j))
                self.wait(e, ("ls1", j))
                self.inc(e.tensor_scalar_mul(t["mvec"], t["s1"], 1.0 / c.D),
                         "VD", 1, ("lmv", j))
                self.wait(e, ("lmv", j))
                self.inc(e.tensor_mul(t["mm"], t["mvec"], t["mvec"]), "VD", 1, ("lmm", j))
                self.wait(e, ("lmm", j))
                self.inc(e.scalar_tensor_tensor(t["var"], t["s2"], 1.0 / c.D, t["mm"],
                                                ALU.mult, ALU.subtract),
                         "VD", 1, ("lv0", j))
                self.wait(e, ("lv0", j))
                inst = e.tensor_scalar_add(t["var"], t["var"], LN_EPS)
                self.inc(inst, "VD", 1, ("var", j))
                self.wait(e, ("std", j))
                self.inc(e.reciprocal(t["rvec"], t["std"]), "VD", 1, ("lrv", j))
                self.wait(e, ("lrv", j))
                inst = e.scalar_tensor_tensor(t["bvec"], t["mvec"], -1.0, t["rvec"],
                                              ALU.mult, ALU.mult)
                self.inc(inst, "VD", 1, ("rb", j))
            if en == "act":
                self.wait(e, ("xld", j))
                self.wait(e, ("tpA", j - 1, c.KD - 1))
                inst = e.activation(t["xsq2"], xb, AF.Square, accum_out=t["s2"])
                self.inc(inst, "VA", 1, ("sq", j))
                self.wait(e, ("var", j))
                inst = e.activation(t["std"], t["var"], AF.Sqrt)
                self.inc(inst, "VA", 1, ("std", j))
                self.wait(e, ("rb", j))
                inst = e.activation(t["xntok"], xb, AF.Identity,
                                    bias=t["bvec"], scale=t["rvec"])
                self.inc(inst, "VA", 1, ("xn", j))
                self.mark_here("VA", ("xbuf_free", j))
            for kd in range(c.KD):
                bank = 4 + (atp[0] % 4)
                ev = "dve" if atp[0] % 2 == 0 else "act"
                atp[0] += 1
                self.bank_use(en, e, bank, ("tpAe", j, kd))
                if en == "pe":
                    if kd == 0:
                        if j == 0:
                            self.wait(e, ("consts",))
                        self.wait(e, ("xn", j))
                    inst = e.transpose(t["psum"][bank][:, 0:128],
                                       t["xntok"][:, kd * 128:(kd + 1) * 128],
                                       t["ident"])
                    self.inc(inst, "PE", 1, ("tpA", j, kd))
                if en == ev:
                    self.wait(e, ("tpA", j, kd))
                    dst = t["xn_f"][:, kd * c.Tc + j * 128: kd * c.Tc + (j + 1) * 128]
                    if ev == "dve":
                        inst = e.tensor_copy(dst, t["psum"][bank][:, 0:128])
                        self.inc(inst, "VD", 1, ("tpAe", j, kd))
                    else:
                        inst = e.activation(dst, t["psum"][bank][:, 0:128], AF.Copy)
                        self.inc(inst, "VA", 1, ("tpAe", j, kd))
            if ((j + 1) * 128) % c.TM == 0:
                h2 = ((j + 1) * 128) // c.TM - 1
                if en == "dve":
                    self.mark_here("VD", ("xnf_half", h2))
                if en == "act":
                    self.mark_here("VA", ("xnf_halfA", h2))
        if en == "dve":
            self.mark_here("VD", ("xnf_done",))
        if en == "act":
            self.mark_here("VA", ("xnf_doneA",))

        for wi, wname in enumerate(["wqT", "wkT", "wvT"]):
            for mc in range(c.H):
                it = wi * c.H + mc
                wb = t["wbufA"][it % 2]
                if en == "sp":
                    self.wait(e, ("wA_free", it - 2))
                    src = t[wname][:, mc * 128:(mc + 1) * 128] \
                        .rearrange("(kd p) m -> p kd m", p=128)
                    dst = wb.rearrange("p (kd m) -> p kd m", m=128)
                    self.inc(e.dma_start(out=dst, in_=src), f"swa{it % 2}", 16,
                             ("wA", it))
                for tm in range(c.NTM):
                    bank = (it * c.NTM + tm) % 2
                    ps = t["psum"][bank][:, 0:c.TM]
                    self.bank_use(en, e, bank, ("pev", it, tm))
                    if en == "pe":
                        if tm == 0:
                            self.wait(e, ("wA", it))
                        if it == 0:
                            self.wait(e, ("xnf_half", tm))
                            self.wait(e, ("xnf_halfA", tm))
                        for kd in range(c.KD):
                            inst = e.matmul(
                                ps,
                                wb[:, kd * 128:(kd + 1) * 128],
                                t["xn_f"][:, kd * c.Tc + tm * c.TM:
                                          kd * c.Tc + (tm + 1) * c.TM],
                                start=(kd == 0), stop=(kd == c.KD - 1))
                            self.inc(inst, "PE", 1,
                                     ("proj", it, tm) if kd == c.KD - 1 else None)
                        if tm == c.NTM - 1:
                            self.mark_here("PE", ("wA_free", it))
                    if en == "act":
                        if tm == 0:
                            self.wait(e, ("qkvbuf_free", it - 1))
                        self.wait(e, ("proj", it, tm))
                        dst = t["qkvbuf"][:, tm * c.TM:(tm + 1) * c.TM]
                        inst = e.activation(dst, ps, AF.Copy)
                        self.inc(inst, "VA", 1, ("pev", it, tm))
                if wname in ("wqT", "wkT"):
                    if en == "dve":
                        if it == 0:
                            self.wait(e, ("consts",))
                        self.wait(e, ("pev", it, c.NTM - 1))
                        self.wait(e, ("abuf_free", it - 2))
                        self.wait(e, ("rope", it - 1))
                        self.inc(e.stream_shuffle(t["ropetmp"], t["qkvbuf"],
                                                  mask=[i ^ 1 for i in range(32)]),
                                 "VD", 1, ("rsh", it))
                        self.inc(e.tensor_mul(t["ropebuf"][it % 2], t["qkvbuf"],
                                              t["rope_cos"]), "VD", 1, ("rm1", it))
                        self.wait(e, ("rsh", it))
                        self.inc(e.tensor_mul(t["ropetmp"], t["ropetmp"],
                                              t["rope_sin"]), "VD", 1, ("rm2", it))
                        self.wait(e, ("rm2", it))
                        inst = e.tensor_add(t["ropebuf"][it % 2],
                                            t["ropebuf"][it % 2], t["ropetmp"])
                        self.inc(inst, "VD", 1, ("rope", it))
                        self.mark_here("VD", ("qkvbuf_free", it))
                    if en == "pool":
                        self.wait(e, ("rope", it))
                        dst = (t["q_st"] if wname == "wqT" else t["k_src"])[mc]
                        inst = e.dma_start(out=dst, in_=t["ropebuf"][it % 2])
                        sn = f"sstq{it % 2}" if wname == "wqT" else f"sstk{it % 2}"
                        self.inc(inst, sn, 16, ("stQK", it))
                        self.mark_here(sn, ("abuf_free", it))
                else:
                    for j in range(c.NT):
                        bank = 4 + (atp[0] % 4)
                        ev = "dve" if atp[0] % 2 == 0 else "act"
                        atp[0] += 1
                        self.bank_use(en, e, bank, ("tpVe", it, j))
                        if en == "pe":
                            if j == 0:
                                self.wait(e, ("pev", it, c.NTM - 1))
                            inst = e.transpose(
                                t["psum"][bank][:, 0:128],
                                t["qkvbuf"][:, j * 128:(j + 1) * 128],
                                t["ident"])
                            self.inc(inst, "PE", 1, ("tpV", it, j))
                            if j == c.NT - 1:
                                self.mark_here("PE", ("qkvbuf_free", it))
                                if it == 3 * c.H - 1:
                                    self.mark_here("PE", ("Adone_pe",))
                        if en == ev:
                            if j < 2:
                                self.wait(e, ("abuf_free", it - 2))
                            self.wait(e, ("tpV", it, j))
                            dst = t["vtbuf"][it % 2][:, j * 128:(j + 1) * 128]
                            if ev == "dve":
                                inst = e.tensor_copy(dst, t["psum"][bank][:, 0:128])
                                self.inc(inst, "VD", 1, ("tpVe", it, j))
                            else:
                                inst = e.activation(dst, t["psum"][bank][:, 0:128],
                                                    AF.Copy)
                                self.inc(inst, "VA", 1, ("tpVe", it, j))
                    if en == "pool":
                        self.wait(e, ("tpVe", it, c.NT - 1))
                        self.wait(e, ("tpVe", it, c.NT - 2))
                        dst = t["v_src"][mc].rearrange("(j p) h -> p j h", p=128)
                        src = t["vtbuf"][it % 2].rearrange("p (j h) -> p j h", h=128)
                        inst = e.dma_start(out=dst, in_=src)
                        self.inc(inst, f"sstv{it % 2}", 16, ("stV", it))
                        self.mark_here(f"sstv{it % 2}", ("abuf_free", it))

        if en == "pool":
            if e.live:
                for sn in ["sstk0", "sstk1", "sstv0", "sstv1"]:
                    e.wait_ge(self.sems[sn].h, self.sems[sn].n)
            inst = e.collective_compute(
                "AllGather", ALU.bypass,
                replica_groups=[[0, 1, 2, 3], [4, 5, 6, 7]],
                ins=[t["k_src_flat"]], outs=[t["k_dst_flat"]])
            self.inc(inst, "sccK", 1, ("ccK",))
            inst = e.collective_compute(
                "AllGather", ALU.bypass,
                replica_groups=[[0, 1, 2, 3], [4, 5, 6, 7]],
                ins=[t["v_src_flat"]], outs=[t["v_dst_flat"]])
            self.inc(inst, "sccV", 1, ("ccV",))

    def phaseB(self, en, e):
        c, t = self.c, self.t
        for g in range(c.NG):
            for ic in range(c.IGC):
                it = g * c.IGC + ic
                gb = t["gbuf"][it % 2]
                ub = t["ubuf1tmp"] if it == 1 else t["ubuf"][it % 2]
                if en == "sp":
                    if it == 0:
                        self.wait(e, ("tpA", c.NT - 1, c.KD - 1))
                        self.wait(e, ("rb", c.NT - 1))
                        self.wait(e, ("xn", c.NT - 1))
                    if it == 3:
                        self.wait(e, ("qkvbuf_free", 3 * c.H - 1))
                        self.wait(e, ("rope", 2 * c.H - 1))
                    self.wait(e, ("gu_free", it - 2))
                    col = g * c.IG + ic * 128
                    src = t["wguT"][:, col:col + 128].rearrange("(kd p) m -> p kd m", p=128)
                    self.inc(e.dma_start(out=gb.rearrange("p (kd m) -> p kd m", m=128),
                                         in_=src), f"sgu{it % 2}", 16, None)
                    srcu = t["wguT"][:, c.I + col:c.I + col + 128] \
                        .rearrange("(kd p) m -> p kd m", p=128)
                    self.inc(e.dma_start(out=ub.rearrange("p (kd m) -> p kd m", m=128),
                                         in_=srcu), f"sgu{it % 2}", 16, ("guld", it))
                for tm in range(c.NTM):
                    L = it * c.NTM + tm
                    bg = (2 * L) % 4
                    bu = bg + 1
                    psg = t["psum"][bg][:, 0:c.TM]
                    psu = t["psum"][bu][:, 0:c.TM]
                    self.bank_use(en, e, bg, ("silu", it, tm))
                    self.bank_use(en, e, bu, ("hslab", it, tm))
                    if en == "pe":
                        if tm == 0:
                            self.wait(e, ("guld", it))
                            if it == 0:
                                self.wait(e, ("xnf_done",))
                                self.wait(e, ("xnf_doneA",))
                        for kd in range(c.KD):
                            rhs = t["xn_f"][:, kd * c.Tc + tm * c.TM:
                                            kd * c.Tc + (tm + 1) * c.TM]
                            inst = e.matmul(psg, gb[:, kd * 128:(kd + 1) * 128], rhs,
                                            start=(kd == 0), stop=(kd == c.KD - 1))
                            self.inc(inst, "PE", 1, None)
                        for kd in range(c.KD):
                            rhs = t["xn_f"][:, kd * c.Tc + tm * c.TM:
                                            kd * c.Tc + (tm + 1) * c.TM]
                            inst = e.matmul(psu, ub[:, kd * 128:(kd + 1) * 128], rhs,
                                            start=(kd == 0), stop=(kd == c.KD - 1))
                            self.inc(inst, "PE", 1,
                                     ("gu", it, tm) if kd == c.KD - 1 else None)
                        if tm == c.NTM - 1:
                            self.mark_here("PE", ("gu_free", it))
                        if it == c.NG * c.IGC - 1 and tm == c.NTM - 1:
                            self.mark_here("PE", ("gu_alldone",))
                    if en == "act":
                        self.wait(e, ("gu", it, tm))
                        self.wait(e, ("hslab_prev", it - 1, tm))
                        dst = t["silubuf"][:, tm * c.TM:(tm + 1) * c.TM]
                        inst = e.activation(dst, psg, AF.Silu)
                        self.inc(inst, "VA", 1, ("silu", it, tm))
                    if en == "dve":
                        self.wait(e, ("silu", it, tm))
                        self.wait(e, ("downdone", g - 1))
                        dst = t["h_g"][:, ic * c.Tc + tm * c.TM:
                                       ic * c.Tc + (tm + 1) * c.TM]
                        inst = e.tensor_mul(dst, psu,
                                            t["silubuf"][:, tm * c.TM:(tm + 1) * c.TM])
                        self.inc(inst, "VD", 1, ("hslab", it, tm))
                        self.mark_here("VD", ("hslab_prev", it, tm))
            HIC = c.IGC // 2
            HK = HIC * 128
            for hf in range(2):
                icl = g * c.IGC + 2 * hf + HIC - 1
                hcol = hf * HIC * c.Tc
                if en == "dve":
                    self.wait(e, ("hslab", icl, c.NTM - 1))
                    if g == 0:
                        if hf == 0:
                            self.wait(e, ("abuf_free", 3 * c.H - 2))
                            self.wait(e, ("abuf_free", 3 * c.H - 1))
                        else:
                            self.wait(e, ("rope", 2 * c.H - 1))
                    src = t["h_g"][:, hcol:hcol + HIC * c.Tc]
                    inst = e.tensor_copy(t["hg_hi"][hf], src)
                    self.inc(inst, "VD", 1, ("hghi", g, hf))
                    self.wait(e, ("hghi", g, hf))
                    inst = e.tensor_sub(t["hg_lo"][hf], src,
                                        t["hg_hi"][hf].bitcast(F32))
                    self.inc(inst, "VD", 1, ("hglo", g, hf))
                for m2 in range(c.KD):
                    it2 = (2 * g + hf) * c.KD + m2
                    db = t["dbuf"][it2 % 2]
                    dbh = t["dbhi"][it2 % 2]
                    dbl = t["dblo"][it2 % 2]
                    if en == "sp":
                        self.wait(e, ("d_free", it2 - 2))
                        r0 = g * c.IG + hf * HK
                        src = t["wdT"][r0:r0 + HK, m2 * 128:(m2 + 1) * 128] \
                            .rearrange("(kc p) m -> p kc m", p=128)
                        self.inc(e.dma_start(out=db.rearrange("p (kc m) -> p kc m", m=128),
                                             in_=src), f"sd{it2 % 2}", 16, ("dld", it2))
                    if en == "dve":
                        self.wait(e, ("dld", it2))
                        self.wait(e, ("d_free", it2 - 2))
                        inst = e.tensor_copy(dbh, db)
                        self.inc(inst, "VD", 1, ("dbhi", it2))
                        self.wait(e, ("dbhi", it2))
                        inst = e.tensor_sub(dbl, db, dbh.bitcast(F32))
                        self.inc(inst, "VD", 1, ("dblo", it2))
                    for tm in range(c.NTM):
                        bank = 4 + (it2 * c.NTM + tm) % 2
                        ps = t["psum"][bank][:, 0:c.TM]
                        self.bank_use(en, e, bank, ("dacc", it2, tm))
                        if en == "pe":
                            if tm == 0:
                                self.wait(e, ("dblo", it2))
                            if m2 == 0 and tm == 0:
                                self.wait(e, ("hglo", g, hf))
                            i = 0
                            for kc in range(HIC):
                                wh = dbh[:, kc * 128:(kc + 1) * 128]
                                wl = dbl[:, kc * 128:(kc + 1) * 128]
                                xcol = kc * c.Tc + tm * c.TM
                                xh = t["hg_hi"][hf][:, xcol:xcol + c.TM]
                                xl = t["hg_lo"][hf][:, xcol:xcol + c.TM]
                                for lh, rh in ((wh, xh), (wh, xl), (wl, xh)):
                                    inst = e.matmul(ps, lh, rh, start=(i == 0),
                                                    stop=(i == 3 * HIC - 1))
                                    i += 1
                                    self.inc(inst, "PE", 1,
                                             ("down", it2, tm) if i == 3 * HIC else None)
                            if hf == 1 and m2 == c.KD - 1 and tm == c.NTM - 1:
                                self.mark_here("PE", ("downdone", g))
                            if tm == c.NTM - 1:
                                self.mark_here("PE", ("d_free", it2))
                        if en == "dve":
                            self.wait(e, ("down", it2, tm))
                            if it2 >= c.KD:
                                self.wait(e, ("dacc", it2 - c.KD, tm))
                            dst = t["down_acc"][:, m2 * c.Tc + tm * c.TM:
                                                m2 * c.Tc + (tm + 1) * c.TM]
                            if it2 < c.KD:
                                inst = e.tensor_copy(dst, ps)
                            else:
                                inst = e.tensor_add(dst, dst, ps)
                            self.inc(inst, "VD", 1, ("dacc", it2, tm))
        if en == "act":
            self.wait(e, ("dacc", 2 * c.NG * c.KD - 1, c.NTM - 1))
            inst = e.activation(t["down_acc"], t["down_acc"], AF.Silu)
            self.inc(inst, "VA", 1, ("gdone",))

    def phaseC(self, en, e):
        c, t = self.c, self.t
        T = c.H * c.NT
        tpctr = 0

        def S0(idx):
            h, jj = divmod(idx, c.NT)
            j = c.NT - 1 - jj
            kh, vh, qh = t["khbuf"][h % 2], t["vhbuf"][h % 2], t["qhbuf"][h % 2]
            nb = j + 1
            sc = t["scores2"][idx % 2]
            if jj == 0 and en == "pool":
                if h == 0:
                    self.wait(e, ("ccK",))
                    self.wait(e, ("ccV",))
                    if e.live:
                        for sn in ["sstq0", "sstq1"]:
                            e.wait_ge(self.sems[sn].h, self.sems[sn].n)
                if h < 2:
                    self.wait(e, ("gu", c.NG * c.IGC - 1, c.NTM - 1))
                self.wait(e, ("head_done", h - 2))
                if h < 2:
                    self.wait(e, ("downdone", c.NG - 1))
                    self.wait(e, ("hslab", c.NG * c.IGC - 1, c.NTM - 1))
                self.inc(e.dma_start(out=qh, in_=t["q_st"][h]), f"sqh{h % 2}", 16,
                         ("hld", h))
                khr = kh.rearrange("p (j r t) -> p j r t", r=4, t=128)
                vhr = vh.rearrange("p (j r h2) -> p j r h2", r=4, h2=128)
                for r in range(4):
                    src = t["k_dst"][r, h].rearrange("p (j t) -> p j t", t=128)
                    self.inc(e.dma_start(out=khr[:, :, r], in_=src), f"skh{h % 2}", 16,
                             ("khld", h) if r == 3 else None)
                    srcv = t["v_dst"][r, h].rearrange("(j p) h2 -> p j h2", p=128)
                    self.inc(e.dma_start(out=vhr[:, :, r], in_=srcv), f"svh{h % 2}", 16,
                             ("vhld", h) if r == 3 else None)
            for b in range(nb):
                bank = [0, 1, 2 + ((idx + 1) % 2)][b % 3]
                ps = t["psum"][bank][:, 0:512]
                self.bank_use(en, e, bank, ("Cse", idx, b))
                if en == "pe":
                    if b == 0 and jj == 0:
                        self.wait(e, ("hld", h))
                        self.wait(e, ("khld", h))
                        self.wait(e, ("vhld", h))
                    inst = e.matmul(ps, qh[:, j * 128:(j + 1) * 128],
                                    kh[:, b * 512:(b + 1) * 512],
                                    start=True, stop=True)
                    self.inc(inst, "PE", 1, ("Cs", idx, b))
                ev = "act" if b < j else "dve"
                if en == ev:
                    if idx < 2 and b in (0, j):
                        self.wait(e, ("abuf_free", 3 * c.H - 2))
                        self.wait(e, ("abuf_free", 3 * c.H - 1))
                    if (b == 0 or b == j) and idx >= 2:
                        self.wait(e, ("scfree", idx))
                    self.wait(e, ("Cs", idx, b))
                    if b < j:
                        inst = e.activation(sc[:, b * 512:(b + 1) * 512], ps, AF.Copy)
                        self.inc(inst, "VA", 1, ("Cse", idx, b))
                    else:
                        if idx == 0:
                            self.wait(e, ("consts",))
                        inst = e.tensor_add(sc[:, b * 512:(b + 1) * 512],
                                            ps, t["mask"])
                        self.inc(inst, "VD", 1, ("Cse", idx, b))
                if en == "dve":
                    self.wait(e, ("Cse", idx, b))
                    if b == 0:
                        self.wait(e, ("Cmax", idx - 2))
                    inst = e.reduce_max(t["bmax2"][idx % 2][:, b:b + 1],
                                        sc[:, b * 512:(b + 1) * 512], axis=AXX)
                    self.inc(inst, "VD", 1, ("Cbm", idx, b))

        def S1max(idx):
            h, jj = divmod(idx, c.NT)
            j = c.NT - 1 - jj
            nb = j + 1
            if en == "dve":
                self.wait(e, ("Cbm", idx, nb - 1))
                self.wait(e, ("Cexp", idx - 2))
                inst = e.reduce_max(t["negmax2"][idx % 2],
                                    t["bmax2"][idx % 2][:, 0:nb],
                                    axis=AXX, negate=True)
                self.inc(inst, "VD", 1, ("Cmax", idx))

        def S1exp(idx):
            h, jj = divmod(idx, c.NT)
            j = c.NT - 1 - jj
            nb = j + 1
            sc = t["scores2"][idx % 2]
            if en == "act":
                self.wait(e, ("Cmax", idx))
                inst = e.activation(sc[:, 0:nb * 512], sc[:, 0:nb * 512], AF.Exp,
                                    bias=t["negmax2"][idx % 2],
                                    accum_out=t["zvec2"][idx % 2])
                self.inc(inst, "VA", 1, ("Cexp", idx))

        def S1norm(idx):
            h, jj = divmod(idx, c.NT)
            j = c.NT - 1 - jj
            nb = j + 1
            sc = t["scores2"][idx % 2]
            if en == "dve":
                self.wait(e, ("Cexp", idx))
                self.wait(e, ("Cnorm", idx - 2))
                self.inc(e.reciprocal(t["rz2"][idx % 2], t["zvec2"][idx % 2]),
                         "VD", 1, ("Crz", idx))
                self.wait(e, ("Crz", idx))
                inst = e.tensor_scalar_mul(sc[:, 0:nb * 512], sc[:, 0:nb * 512],
                                           t["rz2"][idx % 2])
                self.inc(inst, "VD", 1, ("Cnorm", idx))

        def S2(idx):
            nonlocal tpctr
            h, jj = divmod(idx, c.NT)
            j = c.NT - 1 - jj
            nkt = 4 * (j + 1)
            sc = t["scores2"][idx % 2]
            for kt in range(nkt):
                bank = 4 + (tpctr % 4)
                tpctr += 1
                self.bank_use(en, e, bank, ("Cpt", idx, kt))
                if en == "pe":
                    if kt == 0:
                        self.wait(e, ("Cnorm", idx))
                    inst = e.transpose(t["psum"][bank][:, 0:128],
                                       sc[:, kt * 128:(kt + 1) * 128], t["ident"])
                    self.inc(inst, "PE", 1, ("Ct", idx, kt))
                    if kt == nkt - 1:
                        self.mark_here("PE", ("scfree", idx + 2))
                ev = "act" if kt % 2 == 0 else "dve"
                if en == ev:
                    if kt in (0, 1) and idx > 0:
                        self.wait(e, ("ptfree", idx))
                    self.wait(e, ("Ct", idx, kt))
                    dst = t["pt"][:, kt * 128:(kt + 1) * 128]
                    if ev == "act":
                        inst = e.activation(dst, t["psum"][bank][:, 0:128], AF.Copy)
                        self.inc(inst, "VA", 1, ("Cpt", idx, kt))
                    else:
                        inst = e.tensor_copy(dst, t["psum"][bank][:, 0:128])
                        self.inc(inst, "VD", 1, ("Cpt", idx, kt))

        def S3(idx):
            h, jj = divmod(idx, c.NT)
            j = c.NT - 1 - jj
            nkt = 4 * (j + 1)
            vh = t["vhbuf"][h % 2]
            avbank = 2 + (idx % 2)
            psav = t["psum"][avbank][:, 0:128]
            self.bank_use(en, e, avbank, ("Co", idx))
            if en == "pe":
                self.wait(e, ("Cpt", idx, nkt - 2))
                self.wait(e, ("Cpt", idx, nkt - 1))
                for kt in range(nkt):
                    inst = e.matmul(psav, vh[:, kt * 128:(kt + 1) * 128],
                                    t["pt"][:, kt * 128:(kt + 1) * 128],
                                    start=(kt == 0), stop=(kt == nkt - 1))
                    self.inc(inst, "PE", 1,
                             ("Cav", idx) if kt == nkt - 1 else None)
                self.mark_here("PE", ("ptfree", idx + 1))
                if jj == c.NT - 1:
                    self.mark_here("PE", ("head_done", h))
                if idx == T - 1:
                    self.mark_here("PE", ("attn_done",))
            if en == "act":
                self.wait(e, ("Cav", idx))
                if jj == 0:
                    self.wait(e, ("ost", h - 1))
                    if idx == 0:
                        self.wait(e, ("hslab", c.NG * c.IGC - 1, c.NTM - 1))
                inst = e.activation(t["obuf"][:, j * 128:(j + 1) * 128], psav,
                                    AF.Copy)
                self.inc(inst, "VA", 1, ("Co", idx))
            if jj == c.NT - 1 and en == "pool":
                for jj in range(c.NT):
                    self.wait(e, ("Co", h * c.NT + jj))
                inst = e.dma_start(out=t["o_st"][h], in_=t["obuf"])
                self.inc(inst, "ssto", 16, ("ost", h))

        for idx in range(T + 1):
            if idx >= 1:
                S1exp(idx - 1)
                S1norm(idx - 1)
            if idx < T:
                S0(idx)
                S1max(idx)
            if idx >= 1:
                S2(idx - 1)
                S3(idx - 1)

    def phaseD(self, en, e):
        c, t = self.c, self.t
        if en == "pool":
            early = [h for h in range(c.H) if (h // 4) % 2 == 0]
            late = [h for h in range(c.H) if (h // 4) % 2 == 1]
            for h in early:
                self.wait(e, ("head_done", c.H - 2))
                self.wait(e, ("ost", h))
                dst = t["o_f"][:, h * c.Tc:(h + 1) * c.Tc]
                self.inc(e.dma_start(out=dst, in_=t["o_st"][h]), "sof", 16, None)
            self.mark_here("sof", ("oearly",))
            for h in late:
                self.wait(e, ("attn_done",))
                self.wait(e, ("ost", h))
                dst = t["o_f"][:, h * c.Tc:(h + 1) * c.Tc]
                self.inc(e.dma_start(out=dst, in_=t["o_st"][h]), "sof2", 16, None)
            self.mark_here("sof2", ("oallld",))
        for mc in range(c.KD):
            wb = t["wobuf"][mc % 2]
            if en == "sp":
                if mc < 2:
                    self.wait(e, ("scfree", c.H * c.NT))
                self.wait(e, ("wo_free", mc - 2))
                src = t["woT"][:, mc * 128:(mc + 1) * 128] \
                    .rearrange("(kd p) m -> p kd m", p=128)
                self.inc(e.dma_start(out=wb.rearrange("p (kd m) -> p kd m", m=128),
                                     in_=src), f"swo{mc % 2}", 16, ("wold", mc))
            for tm in range(c.NTM):
                bank = (mc * c.NTM + tm) % 2
                ps = t["psum"][bank][:, 0:c.TM]
                self.bank_use(en, e, bank, ("Dsg", mc, tm))
                if en == "pe":
                    hseq = ([h for h in range(c.H) if (h // 4) % 2 == 0]
                            + [h for h in range(c.H) if (h // 4) % 2 == 1])
                    if tm == 0:
                        self.wait(e, ("wold", mc))
                    if mc == 0 and tm == 0:
                        self.wait(e, ("oearly",))
                    for i, h in enumerate(hseq):
                        if i == len(hseq) - c.H // 2 and mc == 0 and tm == 0:
                            self.wait(e, ("oallld",))
                        inst = e.matmul(ps, wb[:, h * 128:(h + 1) * 128],
                                        t["o_f"][:, h * c.Tc + tm * c.TM:
                                                 h * c.Tc + (tm + 1) * c.TM],
                                        start=(i == 0), stop=(i == c.H - 1))
                        self.inc(inst, "PE", 1,
                                 ("Dwo", mc, tm) if i == c.H - 1 else None)
                    if tm == c.NTM - 1:
                        self.mark_here("PE", ("wo_free", mc))
                if en == "dve":
                    self.wait(e, ("Dwo", mc, tm))
                    if mc == 0 and tm == 0:
                        self.wait(e, ("gdone",))
                    sl_ = t["down_acc"][:, mc * c.Tc + tm * c.TM:
                                        mc * c.Tc + (tm + 1) * c.TM]
                    inst = e.tensor_mul(sl_, ps, sl_)
                    self.inc(inst, "VD", 1, ("Dsg", mc, tm))
        for j in range(c.NT):
            xb = t["xbuf2"][j % 2]
            if en == "sp":
                if j < 2:
                    self.wait(e, ("scfree", c.H * c.NT + 1))
                self.wait(e, ("x2free", j - 2))
                self.inc(e.dma_start(out=xb, in_=t["x_own"][j]), f"sx2{j % 2}", 16,
                         ("x2", j))
            for kd in range(c.KD):
                bank = 4 + ((j * c.KD + kd) % 4)
                ev = "act" if kd % 2 == 0 else "dve"
                self.bank_use(en, e, bank, ("tpDe", j, kd))
                if en == "pe":
                    self.wait(e, ("Dsg", kd, (j * 128) // c.TM))
                    inst = e.transpose(
                        t["psum"][bank][:, 0:128],
                        t["down_acc"][:, kd * c.Tc + j * 128:kd * c.Tc + (j + 1) * 128],
                        t["ident"])
                    self.inc(inst, "PE", 1, ("tpD", j, kd))
                if en == ev:
                    if j < 2 and kd < 2:
                        self.wait(e, ("ost", c.H - 1))
                        self.wait(e, ("attn_done",))
                    if kd < 2:
                        self.wait(e, ("ytile_free", j - 2))
                    self.wait(e, ("tpD", j, kd))
                    dst = t["ybuf"][j % 2][:, kd * 128:(kd + 1) * 128]
                    if ev == "act":
                        inst = e.activation(dst, t["psum"][bank][:, 0:128], AF.Copy)
                        self.inc(inst, "VA", 1, ("tpDe", j, kd))
                    else:
                        inst = e.tensor_copy(dst, t["psum"][bank][:, 0:128])
                        self.inc(inst, "VD", 1, ("tpDe", j, kd))
            if en == "dve":
                self.wait(e, ("x2", j))
                for kd in range(c.KD):
                    self.wait(e, ("tpDe", j, kd))
                inst = e.tensor_add(t["ybuf"][j % 2], t["ybuf"][j % 2], xb)
                self.inc(inst, "VD", 1, ("yt", j))
                self.mark_here("VD", ("x2free", j))
            if en == "pool":
                self.wait(e, ("yt", j))
                inst = e.dma_start(out=t["y"][j], in_=t["ybuf"][j % 2])
                self.inc(inst, f"so{j % 2}", 16, ("yst", j))
                self.mark_here(f"so{j % 2}", ("ytile_free", j))
        if en == "pool" and e.live:
            e.wait_ge(self.sems["so0"].h, self.sems["so0"].n)
            e.wait_ge(self.sems["so1"].h, self.sems["so1"].n)

    def run_all(self, en, eng):
        for ph in [self.phaseA, self.phaseB, self.phaseC, self.phaseD]:
            ph(en, Px(eng))


def build_nc(cfg: Cfg):
    c = cfg
    nc = bass.Bass("TRN2", target_bir_lowering=False, num_devices=8)
    t = {}
    t["x_own"] = nc.dram_tensor("x_own", [c.NT, 128, c.D], F32, kind="ExternalInput").ap()
    for n, shp in [("wqT", [c.D, c.D]), ("wkT", [c.D, c.D]), ("wvT", [c.D, c.D]),
                   ("woT", [c.D, c.D]), ("wguT", [c.D, 2 * c.I]), ("wdT", [c.I, c.D])]:
        t[n] = nc.dram_tensor(n, shp, F32, kind="ExternalInput").ap()
    t["rope_cos_in"] = nc.dram_tensor("rope_cos_in", [128, c.Tc], F32, kind="ExternalInput").ap()
    t["rope_sin_in"] = nc.dram_tensor("rope_sin_in", [128, c.Tc], F32, kind="ExternalInput").ap()
    t["mask_in"] = nc.dram_tensor("mask_in", [128, 512], F32, kind="ExternalInput").ap()
    t["ident_in"] = nc.dram_tensor("ident_in", [128, 128], F32, kind="ExternalInput").ap()
    t["y"] = nc.dram_tensor("y", [c.NT, 128, c.D], F32, kind="ExternalOutput").ap()
    ksz = c.H * 128 * c.Tc
    t["k_src_flat"] = nc.dram_tensor("k_src", [ksz], F32, kind="Internal").ap()
    t["v_src_flat"] = nc.dram_tensor("v_src", [ksz], F32, kind="Internal").ap()
    t["k_dst_flat"] = nc.dram_tensor("k_dst", [4 * ksz], F32, kind="Internal").ap()
    t["v_dst_flat"] = nc.dram_tensor("v_dst", [4 * ksz], F32, kind="Internal").ap()
    t["k_src"] = t["k_src_flat"].rearrange("(h p t) -> h p t", h=c.H, p=128)
    t["v_src"] = t["v_src_flat"].rearrange("(h t p) -> h t p", h=c.H, p=128)
    t["k_dst"] = t["k_dst_flat"].rearrange("(r h p t) -> r h p t", r=4, h=c.H, p=128)
    t["v_dst"] = t["v_dst_flat"].rearrange("(r h t p) -> r h t p", r=4, h=c.H, p=128)
    t["q_st"] = nc.dram_tensor("q_st", [c.H, 128, c.Tc], F32, kind="Internal").ap()
    t["o_st"] = nc.dram_tensor("o_st", [c.H, 128, c.Tc], F32, kind="Internal").ap()

    KT = c.KD * c.Tc
    szA = 4 * c.Tc + 3 * c.D
    szB = szA + 2 * c.IGC * 128 + c.IGC * c.Tc + c.Tc
    szC = 3 * c.S + 3 * c.Tc
    szD = 2 * c.KD * 128 + 4 * c.D
    assert c.IGC == 4
    HIC = c.IGC // 2
    HK = HIC * 128
    sizes = {"R1": max(KT, 4 * c.S, c.H * c.Tc),
             "R2": max(KT, 3 * c.KD * 128),
             "R3": max(szA, szB, szC, szD),
             "R4": 2 * c.Tc + 512 + 128 + 64,
             "R5": 2 * HK}
    total = sum(sizes.values())
    arena = nc.alloc_sbuf_tensor("arena", [128, total], F32).ap()
    arena_addr = None
    for al in nc.m.functions[0].allocations:
        if isinstance(al, mybir.MemoryLocationSet):
            for ml in al.memorylocations:
                if ml.name == "arena":
                    arena_addr = ml.addr
    assert arena_addr is not None
    off = {}
    o = 0
    for k, v in sizes.items():
        off[k] = o
        o += v

    def sl(base, start, ln):
        return arena[:, off[base] + start: off[base] + start + ln]

    def slr(name, base, start, ln):
        h = nc.alloc_sbuf_tensor_at(name, [128, ln], F32R,
                                    offset=arena_addr + 4 * (off[base] + start))
        return h.ap()

    t["xn_f"] = sl("R1", 0, KT)
    t["khbuf"] = [sl("R1", 0, c.S), sl("R1", c.S, c.S)]
    t["vhbuf"] = [sl("R1", 2 * c.S, c.S), sl("R1", 3 * c.S, c.S)]
    t["o_f"] = sl("R1", 0, c.H * c.Tc)
    t["down_acc"] = sl("R2", 0, KT)
    t["wbufA"] = [sl("R2", 0, c.KD * 128), sl("R2", c.KD * 128, c.KD * 128)]
    t["ubuf1tmp"] = sl("R2", 2 * c.KD * 128, c.KD * 128)
    t["qkvbuf"] = sl("R3", 0, c.Tc)
    t["ropetmp"] = sl("R3", c.Tc, c.Tc)
    t["ropebuf"] = [sl("R3", 2 * c.Tc, c.Tc), sl("R3", 3 * c.Tc, c.Tc)]
    t["vtbuf"] = t["ropebuf"]
    t["xbuf"] = [sl("R3", 4 * c.Tc, c.D), sl("R3", 4 * c.Tc + c.D, c.D)]
    t["xsq2"] = sl("R3", 4 * c.Tc + 2 * c.D, c.D)
    t["xntok"] = t["xsq2"]
    t["gbuf"] = [sl("R3", 4 * c.Tc, c.D), sl("R3", 4 * c.Tc + c.D, c.D)]
    t["ubuf"] = [sl("R3", 4 * c.Tc + 2 * c.D, c.D), sl("R3", 0, c.D)]
    b0 = szA
    t["dbuf"] = [sl("R3", b0, HK), sl("R3", b0 + HK, HK)]
    t["dblo"] = [slr("dblo0", "R3", b0, HK), slr("dblo1", "R3", b0 + HK, HK)]
    t["dbhi"] = [slr("dbhi0", "R5", 0, HK), slr("dbhi1", "R5", HK, HK)]
    t["h_g"] = sl("R3", b0 + 2 * c.IGC * 128, c.IGC * c.Tc)
    hg0 = b0 + 2 * c.IGC * 128
    t["hg_lo"] = [slr("hglo0", "R3", hg0, HIC * c.Tc),
                  slr("hglo1", "R3", hg0 + HIC * c.Tc, HIC * c.Tc)]
    t["hg_hi"] = [slr("hghi0", "R3", 2 * c.Tc, HIC * c.Tc),
                  slr("hghi1", "R4", 0, HIC * c.Tc)]
    t["silubuf"] = sl("R3", b0 + 2 * c.IGC * 128 + c.IGC * c.Tc, c.Tc)
    t["scores2"] = [sl("R3", 0, c.S), sl("R3", c.S, c.S)]
    t["pt"] = sl("R3", 2 * c.S, c.S)
    t["qhbuf"] = [sl("R3", 3 * c.S, c.Tc), sl("R3", 3 * c.S + c.Tc, c.Tc)]
    t["obuf"] = sl("R3", 3 * c.S + 2 * c.Tc, c.Tc)
    t["wobuf"] = [sl("R3", 0, c.KD * 128), sl("R3", c.KD * 128, c.KD * 128)]
    d0 = 2 * c.KD * 128
    t["xbuf2"] = [sl("R3", d0, c.D), sl("R3", d0 + c.D, c.D)]
    t["ybuf"] = [sl("R3", d0 + 2 * c.D, c.D), sl("R3", d0 + 3 * c.D, c.D)]
    t["rope_cos"] = sl("R4", 0, c.Tc)
    t["rope_sin"] = sl("R4", c.Tc, c.Tc)
    t["mask"] = sl("R4", 2 * c.Tc, 512)
    t["ident"] = sl("R4", 2 * c.Tc + 512, 128)
    v0 = 2 * c.Tc + 512 + 128
    for i, n in enumerate(["s1", "s2", "mvec", "mm", "var", "std", "rvec", "bvec",
                           "negmax0", "negmax1", "zvec0", "zvec1", "rz0", "rz1"]):
        t[n] = sl("R4", v0 + i * 2, 1)
    t["negmax2"] = [t["negmax0"], t["negmax1"]]
    t["zvec2"] = [t["zvec0"], t["zvec1"]]
    t["rz2"] = [t["rz0"], t["rz1"]]
    t["bmax2"] = [sl("R4", v0 + 28, c.NT), sl("R4", v0 + 28 + c.NT, c.NT)]

    with ExitStack() as stx:
        psum = [stx.enter_context(nc.psum_tensor(f"ps{i}", [128, 512], F32))
                for i in range(8)]
        t["psum"] = psum
        semnames = ["sc", "sx0", "sx1", "swa0", "swa1", "sstq0", "sstq1",
                    "sstk0", "sstk1", "sstv0", "sstv1", "sgu0", "sgu1",
                    "sd0", "sd1", "sccK", "sccV", "skh0", "skh1", "svh0",
                    "svh1", "sqh0", "sqh1", "ssto", "sof", "sof2", "swo0",
                    "swo1", "sx20", "sx21", "so0", "so1", "PE", "VA", "VD"]
        sems = {n: Sem(n, stx.enter_context(nc.semaphore(n))) for n in semnames}

        marks = {}
        for en in ["sp", "pe", "act", "dve", "pool"]:
            for s in sems.values():
                s.n = 0
            b = Builder(cfg, t, sems, marks)
            b.run_all(en, None)
        def mk(en):
            def emit(eng):
                for s in sems.values():
                    s.n = 0
                b = Builder(cfg, t, sems, marks)
                b.final = True
                b.run_all(en, eng)
            return emit

        with nc.Block() as block:
            block.sync(mk("sp"))
            block.tensor(mk("pe"))
            block.scalar(mk("act"))
            block.vector(mk("dve"))
            block.gpsimd(mk("pool"))
    return nc


def _host_inputs(cfg: Cfg, x, wq, wk, wv, wo, wgu, wd):
    c = cfg
    xs = np.ascontiguousarray(np.asarray(x, dtype=np.float32))
    wqT = np.ascontiguousarray(np.asarray(wq, dtype=np.float32).T)
    wkT = np.ascontiguousarray(np.asarray(wk, dtype=np.float32).T)
    wvT = np.ascontiguousarray(np.asarray(wv, dtype=np.float32).T)
    woT = np.ascontiguousarray(np.asarray(wo, dtype=np.float32).T)
    wguT = np.ascontiguousarray(np.asarray(wgu, dtype=np.float32).T)
    wdT = np.ascontiguousarray(np.asarray(wd, dtype=np.float32).T)
    ident = np.eye(128, dtype=np.float32)
    c128 = 1.0 / (128.0 ** 0.25)
    inv_freq = 1.0 / (10000.0 ** (np.arange(0, 128, 2, dtype=np.float64) / 128.0))
    in_maps = []
    for core in range(8):
        bb, r = core // 4, core % 4
        tiles = [4 * j + r for j in range(c.NT)]
        x_own = np.stack([xs[bb, 128 * g:128 * (g + 1), :] for g in tiles])
        pos = np.concatenate([128 * g + np.arange(128) for g in tiles]).astype(np.float64)
        ang = pos[None, :] * inv_freq[:, None]
        cos = np.cos(ang)
        sin = np.sin(ang)
        rope_cos = np.empty((128, c.Tc), dtype=np.float32)
        rope_sin = np.empty((128, c.Tc), dtype=np.float32)
        rope_cos[0::2] = cos * c128
        rope_cos[1::2] = cos * c128
        rope_sin[0::2] = -sin * c128
        rope_sin[1::2] = sin * c128
        mask = np.full((128, 512), NEG_BIG, dtype=np.float32)
        for i in range(128):
            mask[i, :128 * r + i + 1] = 0.0
        in_maps.append({
            "x_own": x_own, "wqT": wqT, "wkT": wkT, "wvT": wvT, "woT": woT,
            "wguT": wguT, "wdT": wdT, "rope_cos_in": rope_cos,
            "rope_sin_in": rope_sin, "mask_in": mask, "ident_in": ident,
        })
    return in_maps


def _assemble(cfg: Cfg, results):
    c = cfg
    out = np.empty((2, c.S, c.D), dtype=np.float32)
    for core in range(8):
        bb, r = core // 4, core % 4
        y = np.asarray(results[core]["y"])
        for j in range(c.NT):
            g = 4 * j + r
            out[bb, 128 * g:128 * (g + 1), :] = y[j]
    return out


def kernel(L0_attn_x, L0_attn_Wq, L0_attn_Wk, L0_attn_Wv, L0_attn_Wo,
           L0_ffn_w_gate_up, L0_ffn_w_down):
    cfg = Cfg()
    in_maps = _host_inputs(cfg, L0_attn_x, L0_attn_Wq, L0_attn_Wk, L0_attn_Wv,
                           L0_attn_Wo, L0_ffn_w_gate_up, L0_ffn_w_down)
    nc = build_nc(cfg)
    res = run_bass_kernel_spmd(nc, in_maps, core_ids=list(range(8)))
    return _assemble(cfg, res.results)


# version 17
# speedup vs baseline: 1.0814x; 1.0814x over previous
import numpy as np
from contextlib import ExitStack

import concourse.bass as bass
import concourse.mybir as mybir
from concourse.bass_utils import run_bass_kernel_spmd

F32 = mybir.dt.float32
F32R = mybir.dt.float32r
AF = mybir.ActivationFunctionType
ALU = mybir.AluOpType
AXX = mybir.AxisListType.X

NEG_BIG = -1.0e30
LN_EPS = 1e-5


class Cfg:
    def __init__(self, D=2048, H=16, S=4096, INTER=8192, NG=16):
        self.D, self.H, self.S, self.I = D, H, S, INTER
        assert D == H * 128
        self.KD = D // 128
        self.Tc = S // 4
        self.NT = self.Tc // 128
        assert self.NT == S // 512
        self.TM = min(512, self.Tc)
        self.NTM = self.Tc // self.TM
        self.NKT = S // 128
        self.NG = NG
        self.IG = INTER // NG
        self.IGC = self.IG // 128
        assert self.IG * NG == INTER and self.IGC * 128 == self.IG


class Px:

    def __init__(self, eng):
        self._e = eng

    @property
    def live(self):
        return self._e is not None

    def __getattr__(self, name):
        if self._e is None:
            return lambda *a, **k: None
        return getattr(self._e, name)


class Sem:
    def __init__(self, name, handle=None):
        self.name, self.h, self.n = name, handle, 0


class Builder:

    def __init__(self, cfg, tensors, sems, marks):
        self.c = cfg
        self.t = tensors
        self.sems = sems
        self.marks = marks
        self.banks = {}
        self.final = False
        self.hiwater = {}

    def inc(self, inst, semname, k, key=None):
        s = self.sems[semname]
        s.n += k
        if inst is not None:
            inst.then_inc(s.h, k)
        if key is not None:
            if not self.final:
                self.marks[key] = (semname, s.n)
            else:
                assert self.marks[key] == (semname, s.n), (key, self.marks[key], (semname, s.n))
        return key

    def mark_here(self, semname, key):
        if not self.final:
            self.marks[key] = (semname, self.sems[semname].n)
        else:
            assert self.marks[key] == (semname, self.sems[semname].n), key
        return key

    def wait(self, eng, key):
        if key is None:
            return
        if key not in self.marks:
            neg = any(isinstance(x, int) and x < 0 for x in key)
            assert neg or not self.final, f"missing mark {key}"
            return
        semname, v = self.marks[key]
        if v <= self.hiwater.get(semname, 0):
            return
        self.hiwater[semname] = v
        if eng.live:
            eng.wait_ge(self.sems[semname].h, v)

    def bank_use(self, en, eng, bank, evac_key):
        prev = self.banks.get(bank)
        if en == "pe" and prev is not None:
            self.wait(eng, prev)
        self.banks[bank] = evac_key

    def phaseA(self, en, e):
        c, t = self.c, self.t
        atp = [0]
        if en == "sp":
            for nm_in, nm in [("rope_cos_in", "rope_cos"), ("rope_sin_in", "rope_sin"),
                              ("mask_in", "mask"), ("ident_in", "ident")]:
                inst = e.dma_start(out=t[nm], in_=t[nm_in])
                self.inc(inst, "sc", 16)
            self.mark_here("sc", ("consts",))
        for j in range(c.NT):
            xb = t["xbuf"][j % 2]
            if en == "sp":
                self.wait(e, ("xbuf_free", j - 2))
                self.wait(e, ("rb", j - 2))
                self.inc(e.dma_start(out=xb, in_=t["x_own"][j]), f"sx{j % 2}", 16,
                         ("xld", j))
            if en == "dve":
                self.wait(e, ("xld", j))
                self.inc(e.reduce_sum(t["s1"], xb, axis=AXX), "VD", 1, ("ls1", j))
                self.wait(e, ("sq", j))
                self.wait(e, ("ls1", j))
                self.inc(e.tensor_scalar_mul(t["mvec"], t["s1"], 1.0 / c.D),
                         "VD", 1, ("lmv", j))
                self.wait(e, ("lmv", j))
                self.inc(e.tensor_mul(t["mm"], t["mvec"], t["mvec"]), "VD", 1, ("lmm", j))
                self.wait(e, ("lmm", j))
                self.inc(e.scalar_tensor_tensor(t["var"], t["s2"], 1.0 / c.D, t["mm"],
                                                ALU.mult, ALU.subtract),
                         "VD", 1, ("lv0", j))
                self.wait(e, ("lv0", j))
                inst = e.tensor_scalar_add(t["var"], t["var"], LN_EPS)
                self.inc(inst, "VD", 1, ("var", j))
                self.wait(e, ("std", j))
                self.inc(e.reciprocal(t["rvec"], t["std"]), "VD", 1, ("lrv", j))
                self.wait(e, ("lrv", j))
                inst = e.scalar_tensor_tensor(t["bvec"], t["mvec"], -1.0, t["rvec"],
                                              ALU.mult, ALU.mult)
                self.inc(inst, "VD", 1, ("rb", j))
            if en == "act":
                self.wait(e, ("xld", j))
                self.wait(e, ("tpA", j - 1, c.KD - 1))
                inst = e.activation(t["xsq2"], xb, AF.Square, accum_out=t["s2"])
                self.inc(inst, "VA", 1, ("sq", j))
                self.wait(e, ("var", j))
                inst = e.activation(t["std"], t["var"], AF.Sqrt)
                self.inc(inst, "VA", 1, ("std", j))
                self.wait(e, ("rb", j))
                inst = e.activation(t["xntok"], xb, AF.Identity,
                                    bias=t["bvec"], scale=t["rvec"])
                self.inc(inst, "VA", 1, ("xn", j))
                self.mark_here("VA", ("xbuf_free", j))
            for kd in range(c.KD):
                bank = 4 + (atp[0] % 4)
                ev = "dve" if atp[0] % 2 == 0 else "act"
                atp[0] += 1
                self.bank_use(en, e, bank, ("tpAe", j, kd))
                if en == "pe":
                    if kd == 0:
                        if j == 0:
                            self.wait(e, ("consts",))
                        self.wait(e, ("xn", j))
                    inst = e.transpose(t["psum"][bank][:, 0:128],
                                       t["xntok"][:, kd * 128:(kd + 1) * 128],
                                       t["ident"])
                    self.inc(inst, "PE", 1, ("tpA", j, kd))
                if en == ev:
                    self.wait(e, ("tpA", j, kd))
                    dst = t["xn_f"][:, kd * c.Tc + j * 128: kd * c.Tc + (j + 1) * 128]
                    if ev == "dve":
                        inst = e.tensor_copy(dst, t["psum"][bank][:, 0:128])
                        self.inc(inst, "VD", 1, ("tpAe", j, kd))
                    else:
                        inst = e.activation(dst, t["psum"][bank][:, 0:128], AF.Copy)
                        self.inc(inst, "VA", 1, ("tpAe", j, kd))
            if ((j + 1) * 128) % c.TM == 0:
                h2 = ((j + 1) * 128) // c.TM - 1
                if en == "dve":
                    self.mark_here("VD", ("xnf_half", h2))
                if en == "act":
                    self.mark_here("VA", ("xnf_halfA", h2))
        if en == "dve":
            self.mark_here("VD", ("xnf_done",))
        if en == "act":
            self.mark_here("VA", ("xnf_doneA",))

        for wi, wname in enumerate(["wqT", "wkT", "wvT"]):
            for mc in range(c.H):
                it = wi * c.H + mc
                wb = t["wbufA"][it % 2]
                if en == "sp":
                    self.wait(e, ("wA_free", it - 2))
                    src = t[wname][:, mc * 128:(mc + 1) * 128] \
                        .rearrange("(kd p) m -> p kd m", p=128)
                    dst = wb.rearrange("p (kd m) -> p kd m", m=128)
                    self.inc(e.dma_start(out=dst, in_=src), f"swa{it % 2}", 16,
                             ("wA", it))
                for tm in range(c.NTM):
                    bank = (it * c.NTM + tm) % 2
                    ps = t["psum"][bank][:, 0:c.TM]
                    self.bank_use(en, e, bank, ("pev", it, tm))
                    if en == "pe":
                        if tm == 0:
                            self.wait(e, ("wA", it))
                        if it == 0:
                            self.wait(e, ("xnf_half", tm))
                            self.wait(e, ("xnf_halfA", tm))
                        for kd in range(c.KD):
                            inst = e.matmul(
                                ps,
                                wb[:, kd * 128:(kd + 1) * 128],
                                t["xn_f"][:, kd * c.Tc + tm * c.TM:
                                          kd * c.Tc + (tm + 1) * c.TM],
                                start=(kd == 0), stop=(kd == c.KD - 1))
                            self.inc(inst, "PE", 1,
                                     ("proj", it, tm) if kd == c.KD - 1 else None)
                        if tm == c.NTM - 1:
                            self.mark_here("PE", ("wA_free", it))
                    if en == "act":
                        if tm == 0:
                            self.wait(e, ("qkvbuf_free", it - 1))
                        self.wait(e, ("proj", it, tm))
                        dst = t["qkvbuf"][:, tm * c.TM:(tm + 1) * c.TM]
                        inst = e.activation(dst, ps, AF.Copy)
                        self.inc(inst, "VA", 1, ("pev", it, tm))
                if wname in ("wqT", "wkT"):
                    if en == "dve":
                        if it == 0:
                            self.wait(e, ("consts",))
                        self.wait(e, ("pev", it, c.NTM - 1))
                        self.wait(e, ("abuf_free", it - 2))
                        self.wait(e, ("rope", it - 1))
                        self.inc(e.stream_shuffle(t["ropetmp"], t["qkvbuf"],
                                                  mask=[i ^ 1 for i in range(32)]),
                                 "VD", 1, ("rsh", it))
                        self.inc(e.tensor_mul(t["ropebuf"][it % 2], t["qkvbuf"],
                                              t["rope_cos"]), "VD", 1, ("rm1", it))
                        self.wait(e, ("rsh", it))
                        self.inc(e.tensor_mul(t["ropetmp"], t["ropetmp"],
                                              t["rope_sin"]), "VD", 1, ("rm2", it))
                        self.wait(e, ("rm2", it))
                        inst = e.tensor_add(t["ropebuf"][it % 2],
                                            t["ropebuf"][it % 2], t["ropetmp"])
                        self.inc(inst, "VD", 1, ("rope", it))
                        self.mark_here("VD", ("qkvbuf_free", it))
                    if en == "pool":
                        self.wait(e, ("rope", it))
                        dst = (t["q_st"] if wname == "wqT" else t["k_src"])[mc]
                        inst = e.dma_start(out=dst, in_=t["ropebuf"][it % 2])
                        sn = f"sstq{it % 2}" if wname == "wqT" else f"sstk{it % 2}"
                        self.inc(inst, sn, 16, ("stQK", it))
                        self.mark_here(sn, ("abuf_free", it))
                else:
                    for j in range(c.NT):
                        bank = 4 + (atp[0] % 4)
                        ev = "dve" if atp[0] % 2 == 0 else "act"
                        atp[0] += 1
                        self.bank_use(en, e, bank, ("tpVe", it, j))
                        if en == "pe":
                            if j == 0:
                                self.wait(e, ("pev", it, c.NTM - 1))
                            inst = e.transpose(
                                t["psum"][bank][:, 0:128],
                                t["qkvbuf"][:, j * 128:(j + 1) * 128],
                                t["ident"])
                            self.inc(inst, "PE", 1, ("tpV", it, j))
                            if j == c.NT - 1:
                                self.mark_here("PE", ("qkvbuf_free", it))
                                if it == 3 * c.H - 1:
                                    self.mark_here("PE", ("Adone_pe",))
                        if en == ev:
                            if j < 2:
                                self.wait(e, ("abuf_free", it - 2))
                            self.wait(e, ("tpV", it, j))
                            dst = t["vtbuf"][it % 2][:, j * 128:(j + 1) * 128]
                            if ev == "dve":
                                inst = e.tensor_copy(dst, t["psum"][bank][:, 0:128])
                                self.inc(inst, "VD", 1, ("tpVe", it, j))
                            else:
                                inst = e.activation(dst, t["psum"][bank][:, 0:128],
                                                    AF.Copy)
                                self.inc(inst, "VA", 1, ("tpVe", it, j))
                    if en == "pool":
                        self.wait(e, ("tpVe", it, c.NT - 1))
                        self.wait(e, ("tpVe", it, c.NT - 2))
                        dst = t["v_src"][mc].rearrange("(j p) h -> p j h", p=128)
                        src = t["vtbuf"][it % 2].rearrange("p (j h) -> p j h", h=128)
                        inst = e.dma_start(out=dst, in_=src)
                        self.inc(inst, f"sstv{it % 2}", 16, ("stV", it))
                        self.mark_here(f"sstv{it % 2}", ("abuf_free", it))

        if en == "pool":
            if e.live:
                for sn in ["sstk0", "sstk1", "sstv0", "sstv1"]:
                    e.wait_ge(self.sems[sn].h, self.sems[sn].n)
            inst = e.collective_compute(
                "AllGather", ALU.bypass,
                replica_groups=[[0, 1, 2, 3], [4, 5, 6, 7]],
                ins=[t["k_src_flat"]], outs=[t["k_dst_flat"]])
            self.inc(inst, "sccK", 1, ("ccK",))
            inst = e.collective_compute(
                "AllGather", ALU.bypass,
                replica_groups=[[0, 1, 2, 3], [4, 5, 6, 7]],
                ins=[t["v_src_flat"]], outs=[t["v_dst_flat"]])
            self.inc(inst, "sccV", 1, ("ccV",))

    def phaseB(self, en, e):
        c, t = self.c, self.t
        HIC = c.IGC // 2
        HK = HIC * 128
        for g in range(c.NG):
            for ic in range(c.IGC):
                it = g * c.IGC + ic
                gb = t["gbuf"][it % 2]
                ub = t["ubuf1tmp"] if it == 1 else t["ubuf"][it % 2]
                if en == "sp":
                    if it == 0:
                        self.wait(e, ("tpA", c.NT - 1, c.KD - 1))
                        self.wait(e, ("rb", c.NT - 1))
                        self.wait(e, ("xn", c.NT - 1))
                    if it == 3:
                        self.wait(e, ("qkvbuf_free", 3 * c.H - 1))
                        self.wait(e, ("rope", 2 * c.H - 1))
                    self.wait(e, ("gu_free", it - 2))
                    col = g * c.IG + ic * 128
                    src = t["wguT"][:, col:col + 128].rearrange("(kd p) m -> p kd m", p=128)
                    self.inc(e.dma_start(out=gb.rearrange("p (kd m) -> p kd m", m=128),
                                         in_=src), f"sgu{it % 2}", 16, None)
                    srcu = t["wguT"][:, c.I + col:c.I + col + 128] \
                        .rearrange("(kd p) m -> p kd m", p=128)
                    self.inc(e.dma_start(out=ub.rearrange("p (kd m) -> p kd m", m=128),
                                         in_=srcu), f"sgu{it % 2}", 16, ("guld", it))
                for tm in range(c.NTM):
                    L = it * c.NTM + tm
                    bg = (2 * L) % 4
                    bu = bg + 1
                    psg = t["psum"][bg][:, 0:c.TM]
                    psu = t["psum"][bu][:, 0:c.TM]
                    self.bank_use(en, e, bg, ("silu", it, tm))
                    self.bank_use(en, e, bu, ("hslab", it, tm))
                    if en == "pe":
                        if tm == 0:
                            self.wait(e, ("guld", it))
                            if it == 0:
                                self.wait(e, ("xnf_done",))
                                self.wait(e, ("xnf_doneA",))
                        for kd in range(c.KD):
                            rhs = t["xn_f"][:, kd * c.Tc + tm * c.TM:
                                            kd * c.Tc + (tm + 1) * c.TM]
                            inst = e.matmul(psg, gb[:, kd * 128:(kd + 1) * 128], rhs,
                                            start=(kd == 0), stop=(kd == c.KD - 1))
                            self.inc(inst, "PE", 1, None)
                        for kd in range(c.KD):
                            rhs = t["xn_f"][:, kd * c.Tc + tm * c.TM:
                                            kd * c.Tc + (tm + 1) * c.TM]
                            inst = e.matmul(psu, ub[:, kd * 128:(kd + 1) * 128], rhs,
                                            start=(kd == 0), stop=(kd == c.KD - 1))
                            self.inc(inst, "PE", 1,
                                     ("gu", it, tm) if kd == c.KD - 1 else None)
                        if tm == c.NTM - 1:
                            self.mark_here("PE", ("gu_free", it))
                        if it == c.NG * c.IGC - 1 and tm == c.NTM - 1:
                            self.mark_here("PE", ("gu_alldone",))
                    if en == "act":
                        self.wait(e, ("gu", it, tm))
                        self.wait(e, ("hslab_prev", it - 1, tm))
                        dst = t["silubuf"][:, tm * c.TM:(tm + 1) * c.TM]
                        inst = e.activation(dst, psg, AF.Silu)
                        self.inc(inst, "VA", 1, ("silu", it, tm))
                    if en == "dve":
                        self.wait(e, ("silu", it, tm))
                        self.wait(e, ("downdone", g - 1))
                        dst = t["h_g"][:, ic * c.Tc + tm * c.TM:
                                       ic * c.Tc + (tm + 1) * c.TM]
                        inst = e.tensor_mul(dst, psu,
                                            t["silubuf"][:, tm * c.TM:(tm + 1) * c.TM])
                        self.inc(inst, "VD", 1, ("hslab", it, tm))
                        self.mark_here("VD", ("hslab_prev", it, tm))
                        if tm == c.NTM - 1 and ic % HIC == HIC - 1:
                            hf = ic // HIC
                            self.wait(e, ("hslab", it, tm))
                            if g == 0:
                                if hf == 0:
                                    self.wait(e, ("abuf_free", 3 * c.H - 2))
                                    self.wait(e, ("abuf_free", 3 * c.H - 1))
                                else:
                                    self.wait(e, ("rope", 2 * c.H - 1))
                            hcol = hf * HIC * c.Tc
                            src = t["h_g"][:, hcol:hcol + HIC * c.Tc]
                            inst = e.tensor_copy(t["hg_hi"][hf], src)
                            self.inc(inst, "VD", 1, ("hghi", g, hf))
                            self.wait(e, ("hghi", g, hf))
                            inst = e.tensor_sub(t["hg_lo"][hf], src,
                                                t["hg_hi"][hf].bitcast(F32))
                            self.inc(inst, "VD", 1, ("hglo", g, hf))

            def cv(j):
                self.wait(e, ("dld", j))
                self.wait(e, ("d_free", j - 2))
                inst = e.tensor_copy(t["dbhi"][j % 2], t["dbuf"][j % 2])
                self.inc(inst, "VD", 1, ("dbhi", j))
                self.wait(e, ("dbhi", j))
                inst = e.tensor_sub(t["dblo"][j % 2], t["dbuf"][j % 2],
                                    t["dbhi"][j % 2].bitcast(F32))
                self.inc(inst, "VD", 1, ("dblo", j))

            for hf in range(2):
                base = (2 * g + hf) * c.KD
                if en == "dve":
                    cv(base)
                    if c.KD > 1:
                        cv(base + 1)
                for m2 in range(c.KD):
                    it2 = (2 * g + hf) * c.KD + m2
                    db = t["dbuf"][it2 % 2]
                    dbh = t["dbhi"][it2 % 2]
                    dbl = t["dblo"][it2 % 2]
                    if en == "sp":
                        self.wait(e, ("d_free", it2 - 2))
                        r0 = g * c.IG + hf * HK
                        src = t["wdT"][r0:r0 + HK, m2 * 128:(m2 + 1) * 128] \
                            .rearrange("(kc p) m -> p kc m", p=128)
                        self.inc(e.dma_start(out=db.rearrange("p (kc m) -> p kc m", m=128),
                                             in_=src), f"sd{it2 % 2}", 16, ("dld", it2))
                    for tm in range(c.NTM):
                        bank = 4 + (it2 * c.NTM + tm) % 2
                        ps = t["psum"][bank][:, 0:c.TM]
                        self.bank_use(en, e, bank, ("dacc", it2, tm))
                        if en == "pe":
                            if tm == 0:
                                self.wait(e, ("dblo", it2))
                            if m2 == 0 and tm == 0:
                                self.wait(e, ("hglo", g, hf))
                            i = 0
                            for kc in range(HIC):
                                wh = dbh[:, kc * 128:(kc + 1) * 128]
                                wl = dbl[:, kc * 128:(kc + 1) * 128]
                                xcol = kc * c.Tc + tm * c.TM
                                xh = t["hg_hi"][hf][:, xcol:xcol + c.TM]
                                xl = t["hg_lo"][hf][:, xcol:xcol + c.TM]
                                for lh, rh in ((wh, xh), (wh, xl), (wl, xh)):
                                    inst = e.matmul(ps, lh, rh, start=(i == 0),
                                                    stop=(i == 3 * HIC - 1))
                                    i += 1
                                    self.inc(inst, "PE", 1,
                                             ("down", it2, tm) if i == 3 * HIC else None)
                            if hf == 1 and m2 == c.KD - 1 and tm == c.NTM - 1:
                                self.mark_here("PE", ("downdone", g))
                            if tm == c.NTM - 1:
                                self.mark_here("PE", ("d_free", it2))
                        if en == "dve":
                            self.wait(e, ("down", it2, tm))
                            if it2 >= c.KD:
                                self.wait(e, ("dacc", it2 - c.KD, tm))
                            dst = t["down_acc"][:, m2 * c.Tc + tm * c.TM:
                                                m2 * c.Tc + (tm + 1) * c.TM]
                            if it2 < c.KD:
                                inst = e.tensor_copy(dst, ps)
                            else:
                                inst = e.tensor_add(dst, dst, ps)
                            self.inc(inst, "VD", 1, ("dacc", it2, tm))
                    if en == "dve" and m2 + 2 < c.KD:
                        cv(base + m2 + 2)
        if en == "act":
            self.wait(e, ("dacc", 2 * c.NG * c.KD - 1, c.NTM - 1))
            inst = e.activation(t["down_acc"], t["down_acc"], AF.Silu)
            self.inc(inst, "VA", 1, ("gdone",))

    def phaseC(self, en, e):
        c, t = self.c, self.t
        T = c.H * c.NT
        tpctr = 0

        def S0(idx):
            h, jj = divmod(idx, c.NT)
            j = c.NT - 1 - jj
            kh, vh, qh = t["khbuf"][h % 2], t["vhbuf"][h % 2], t["qhbuf"][h % 2]
            nb = j + 1
            sc = t["scores2"][idx % 2]
            if jj == 0 and en == "pool":
                if h == 0:
                    self.wait(e, ("ccK",))
                    self.wait(e, ("ccV",))
                    if e.live:
                        for sn in ["sstq0", "sstq1"]:
                            e.wait_ge(self.sems[sn].h, self.sems[sn].n)
                if h < 2:
                    self.wait(e, ("gu", c.NG * c.IGC - 1, c.NTM - 1))
                self.wait(e, ("head_done", h - 2))
                if h < 2:
                    self.wait(e, ("downdone", c.NG - 1))
                    self.wait(e, ("hslab", c.NG * c.IGC - 1, c.NTM - 1))
                self.inc(e.dma_start(out=qh, in_=t["q_st"][h]), f"sqh{h % 2}", 16,
                         ("hld", h))
                khr = kh.rearrange("p (j r t) -> p j r t", r=4, t=128)
                vhr = vh.rearrange("p (j r h2) -> p j r h2", r=4, h2=128)
                for r in range(4):
                    src = t["k_dst"][r, h].rearrange("p (j t) -> p j t", t=128)
                    self.inc(e.dma_start(out=khr[:, :, r], in_=src), f"skh{h % 2}", 16,
                             ("khld", h) if r == 3 else None)
                    srcv = t["v_dst"][r, h].rearrange("(j p) h2 -> p j h2", p=128)
                    self.inc(e.dma_start(out=vhr[:, :, r], in_=srcv), f"svh{h % 2}", 16,
                             ("vhld", h) if r == 3 else None)
            for b in range(nb):
                bank = [0, 1, 2 + ((idx + 1) % 2)][b % 3]
                ps = t["psum"][bank][:, 0:512]
                self.bank_use(en, e, bank, ("Cse", idx, b))
                if en == "pe":
                    if b == 0 and jj == 0:
                        self.wait(e, ("hld", h))
                        self.wait(e, ("khld", h))
                        self.wait(e, ("vhld", h))
                    inst = e.matmul(ps, qh[:, j * 128:(j + 1) * 128],
                                    kh[:, b * 512:(b + 1) * 512],
                                    start=True, stop=True)
                    self.inc(inst, "PE", 1, ("Cs", idx, b))
                ev = "act" if b < j else "dve"
                if en == ev:
                    if idx < 2 and b in (0, j):
                        self.wait(e, ("abuf_free", 3 * c.H - 2))
                        self.wait(e, ("abuf_free", 3 * c.H - 1))
                    if (b == 0 or b == j) and idx >= 2:
                        self.wait(e, ("scfree", idx))
                    self.wait(e, ("Cs", idx, b))
                    if b < j:
                        inst = e.activation(sc[:, b * 512:(b + 1) * 512], ps, AF.Copy)
                        self.inc(inst, "VA", 1, ("Cse", idx, b))
                    else:
                        if idx == 0:
                            self.wait(e, ("consts",))
                        inst = e.tensor_add(sc[:, b * 512:(b + 1) * 512],
                                            ps, t["mask"])
                        self.inc(inst, "VD", 1, ("Cse", idx, b))
                if en == "dve":
                    self.wait(e, ("Cse", idx, b))
                    if b == 0:
                        self.wait(e, ("Cmax", idx - 2))
                    inst = e.reduce_max(t["bmax2"][idx % 2][:, b:b + 1],
                                        sc[:, b * 512:(b + 1) * 512], axis=AXX)
                    self.inc(inst, "VD", 1, ("Cbm", idx, b))

        def S1max(idx):
            h, jj = divmod(idx, c.NT)
            j = c.NT - 1 - jj
            nb = j + 1
            if en == "dve":
                self.wait(e, ("Cbm", idx, nb - 1))
                self.wait(e, ("Cexp", idx - 2))
                inst = e.reduce_max(t["negmax2"][idx % 2],
                                    t["bmax2"][idx % 2][:, 0:nb],
                                    axis=AXX, negate=True)
                self.inc(inst, "VD", 1, ("Cmax", idx))

        def S1exp(idx):
            h, jj = divmod(idx, c.NT)
            j = c.NT - 1 - jj
            nb = j + 1
            sc = t["scores2"][idx % 2]
            if en == "act":
                self.wait(e, ("Cmax", idx))
                inst = e.activation(sc[:, 0:nb * 512], sc[:, 0:nb * 512], AF.Exp,
                                    bias=t["negmax2"][idx % 2],
                                    accum_out=t["zvec2"][idx % 2])
                self.inc(inst, "VA", 1, ("Cexp", idx))

        def S1norm(idx):
            h, jj = divmod(idx, c.NT)
            j = c.NT - 1 - jj
            nb = j + 1
            sc = t["scores2"][idx % 2]
            if en == "dve":
                self.wait(e, ("Cexp", idx))
                self.wait(e, ("Cnorm", idx - 2))
                self.inc(e.reciprocal(t["rz2"][idx % 2], t["zvec2"][idx % 2]),
                         "VD", 1, ("Crz", idx))
                self.wait(e, ("Crz", idx))
                inst = e.tensor_scalar_mul(sc[:, 0:nb * 512], sc[:, 0:nb * 512],
                                           t["rz2"][idx % 2])
                self.inc(inst, "VD", 1, ("Cnorm", idx))

        def S2(idx):
            nonlocal tpctr
            h, jj = divmod(idx, c.NT)
            j = c.NT - 1 - jj
            nkt = 4 * (j + 1)
            sc = t["scores2"][idx % 2]
            for kt in range(nkt):
                bank = 4 + (tpctr % 4)
                tpctr += 1
                self.bank_use(en, e, bank, ("Cpt", idx, kt))
                if en == "pe":
                    if kt == 0:
                        self.wait(e, ("Cnorm", idx))
                    inst = e.transpose(t["psum"][bank][:, 0:128],
                                       sc[:, kt * 128:(kt + 1) * 128], t["ident"])
                    self.inc(inst, "PE", 1, ("Ct", idx, kt))
                    if kt == nkt - 1:
                        self.mark_here("PE", ("scfree", idx + 2))
                ev = "act" if kt % 2 == 0 else "dve"
                if en == ev:
                    if kt in (0, 1) and idx > 0:
                        self.wait(e, ("ptfree", idx))
                    self.wait(e, ("Ct", idx, kt))
                    dst = t["pt"][:, kt * 128:(kt + 1) * 128]
                    if ev == "act":
                        inst = e.activation(dst, t["psum"][bank][:, 0:128], AF.Copy)
                        self.inc(inst, "VA", 1, ("Cpt", idx, kt))
                    else:
                        inst = e.tensor_copy(dst, t["psum"][bank][:, 0:128])
                        self.inc(inst, "VD", 1, ("Cpt", idx, kt))

        def S3(idx):
            h, jj = divmod(idx, c.NT)
            j = c.NT - 1 - jj
            nkt = 4 * (j + 1)
            vh = t["vhbuf"][h % 2]
            avbank = 2 + (idx % 2)
            psav = t["psum"][avbank][:, 0:128]
            self.bank_use(en, e, avbank, ("Co", idx))
            if en == "pe":
                self.wait(e, ("Cpt", idx, nkt - 2))
                self.wait(e, ("Cpt", idx, nkt - 1))
                for kt in range(nkt):
                    inst = e.matmul(psav, vh[:, kt * 128:(kt + 1) * 128],
                                    t["pt"][:, kt * 128:(kt + 1) * 128],
                                    start=(kt == 0), stop=(kt == nkt - 1))
                    self.inc(inst, "PE", 1,
                             ("Cav", idx) if kt == nkt - 1 else None)
                self.mark_here("PE", ("ptfree", idx + 1))
                if jj == c.NT - 1:
                    self.mark_here("PE", ("head_done", h))
                if idx == T - 1:
                    self.mark_here("PE", ("attn_done",))
            if en == "act":
                self.wait(e, ("Cav", idx))
                if jj == 0:
                    self.wait(e, ("ost", h - 1))
                    if idx == 0:
                        self.wait(e, ("hslab", c.NG * c.IGC - 1, c.NTM - 1))
                inst = e.activation(t["obuf"][:, j * 128:(j + 1) * 128], psav,
                                    AF.Copy)
                self.inc(inst, "VA", 1, ("Co", idx))
            if jj == c.NT - 1 and en == "pool":
                for jj in range(c.NT):
                    self.wait(e, ("Co", h * c.NT + jj))
                inst = e.dma_start(out=t["o_st"][h], in_=t["obuf"])
                self.inc(inst, "ssto", 16, ("ost", h))

        for idx in range(T + 1):
            if idx >= 1:
                S1exp(idx - 1)
                S1norm(idx - 1)
            if idx < T:
                S0(idx)
                S1max(idx)
            if idx >= 1:
                S2(idx - 1)
                S3(idx - 1)

    def phaseD(self, en, e):
        c, t = self.c, self.t
        if en == "pool":
            early = [h for h in range(c.H) if (h // 4) % 2 == 0]
            late = [h for h in range(c.H) if (h // 4) % 2 == 1]
            for h in early:
                self.wait(e, ("head_done", c.H - 2))
                self.wait(e, ("ost", h))
                dst = t["o_f"][:, h * c.Tc:(h + 1) * c.Tc]
                self.inc(e.dma_start(out=dst, in_=t["o_st"][h]), "sof", 16, None)
            self.mark_here("sof", ("oearly",))
            for h in late:
                self.wait(e, ("attn_done",))
                self.wait(e, ("ost", h))
                dst = t["o_f"][:, h * c.Tc:(h + 1) * c.Tc]
                self.inc(e.dma_start(out=dst, in_=t["o_st"][h]), "sof2", 16, None)
            self.mark_here("sof2", ("oallld",))
        for mc in range(c.KD):
            wb = t["wobuf"][mc % 2]
            if en == "sp":
                if mc < 2:
                    self.wait(e, ("scfree", c.H * c.NT))
                self.wait(e, ("wo_free", mc - 2))
                src = t["woT"][:, mc * 128:(mc + 1) * 128] \
                    .rearrange("(kd p) m -> p kd m", p=128)
                self.inc(e.dma_start(out=wb.rearrange("p (kd m) -> p kd m", m=128),
                                     in_=src), f"swo{mc % 2}", 16, ("wold", mc))
            for tm in range(c.NTM):
                bank = (mc * c.NTM + tm) % 2
                ps = t["psum"][bank][:, 0:c.TM]
                self.bank_use(en, e, bank, ("Dsg", mc, tm))
                if en == "pe":
                    hseq = ([h for h in range(c.H) if (h // 4) % 2 == 0]
                            + [h for h in range(c.H) if (h // 4) % 2 == 1])
                    if tm == 0:
                        self.wait(e, ("wold", mc))
                    if mc == 0 and tm == 0:
                        self.wait(e, ("oearly",))
                    for i, h in enumerate(hseq):
                        if i == len(hseq) - c.H // 2 and mc == 0 and tm == 0:
                            self.wait(e, ("oallld",))
                        inst = e.matmul(ps, wb[:, h * 128:(h + 1) * 128],
                                        t["o_f"][:, h * c.Tc + tm * c.TM:
                                                 h * c.Tc + (tm + 1) * c.TM],
                                        start=(i == 0), stop=(i == c.H - 1))
                        self.inc(inst, "PE", 1,
                                 ("Dwo", mc, tm) if i == c.H - 1 else None)
                    if tm == c.NTM - 1:
                        self.mark_here("PE", ("wo_free", mc))
                if en == "dve":
                    self.wait(e, ("Dwo", mc, tm))
                    if mc == 0 and tm == 0:
                        self.wait(e, ("gdone",))
                    sl_ = t["down_acc"][:, mc * c.Tc + tm * c.TM:
                                        mc * c.Tc + (tm + 1) * c.TM]
                    inst = e.tensor_mul(sl_, ps, sl_)
                    self.inc(inst, "VD", 1, ("Dsg", mc, tm))
        for j in range(c.NT):
            xb = t["xbuf2"][j % 2]
            if en == "sp":
                if j < 2:
                    self.wait(e, ("scfree", c.H * c.NT + 1))
                self.wait(e, ("x2free", j - 2))
                self.inc(e.dma_start(out=xb, in_=t["x_own"][j]), f"sx2{j % 2}", 16,
                         ("x2", j))
            for kd in range(c.KD):
                bank = 4 + ((j * c.KD + kd) % 4)
                ev = "act" if kd % 2 == 0 else "dve"
                self.bank_use(en, e, bank, ("tpDe", j, kd))
                if en == "pe":
                    self.wait(e, ("Dsg", kd, (j * 128) // c.TM))
                    inst = e.transpose(
                        t["psum"][bank][:, 0:128],
                        t["down_acc"][:, kd * c.Tc + j * 128:kd * c.Tc + (j + 1) * 128],
                        t["ident"])
                    self.inc(inst, "PE", 1, ("tpD", j, kd))
                if en == ev:
                    if j < 2 and kd < 2:
                        self.wait(e, ("ost", c.H - 1))
                        self.wait(e, ("attn_done",))
                    if kd < 2:
                        self.wait(e, ("ytile_free", j - 2))
                    self.wait(e, ("tpD", j, kd))
                    dst = t["ybuf"][j % 2][:, kd * 128:(kd + 1) * 128]
                    if ev == "act":
                        inst = e.activation(dst, t["psum"][bank][:, 0:128], AF.Copy)
                        self.inc(inst, "VA", 1, ("tpDe", j, kd))
                    else:
                        inst = e.tensor_copy(dst, t["psum"][bank][:, 0:128])
                        self.inc(inst, "VD", 1, ("tpDe", j, kd))
            if en == "dve":
                self.wait(e, ("x2", j))
                for kd in range(c.KD):
                    self.wait(e, ("tpDe", j, kd))
                inst = e.tensor_add(t["ybuf"][j % 2], t["ybuf"][j % 2], xb)
                self.inc(inst, "VD", 1, ("yt", j))
                self.mark_here("VD", ("x2free", j))
            if en == "pool":
                self.wait(e, ("yt", j))
                inst = e.dma_start(out=t["y"][j], in_=t["ybuf"][j % 2])
                self.inc(inst, f"so{j % 2}", 16, ("yst", j))
                self.mark_here(f"so{j % 2}", ("ytile_free", j))
        if en == "pool" and e.live:
            e.wait_ge(self.sems["so0"].h, self.sems["so0"].n)
            e.wait_ge(self.sems["so1"].h, self.sems["so1"].n)

    def run_all(self, en, eng):
        for ph in [self.phaseA, self.phaseB, self.phaseC, self.phaseD]:
            ph(en, Px(eng))


def build_nc(cfg: Cfg):
    c = cfg
    nc = bass.Bass("TRN2", target_bir_lowering=False, num_devices=8)
    t = {}
    t["x_own"] = nc.dram_tensor("x_own", [c.NT, 128, c.D], F32, kind="ExternalInput").ap()
    for n, shp in [("wqT", [c.D, c.D]), ("wkT", [c.D, c.D]), ("wvT", [c.D, c.D]),
                   ("woT", [c.D, c.D]), ("wguT", [c.D, 2 * c.I]), ("wdT", [c.I, c.D])]:
        t[n] = nc.dram_tensor(n, shp, F32, kind="ExternalInput").ap()
    t["rope_cos_in"] = nc.dram_tensor("rope_cos_in", [128, c.Tc], F32, kind="ExternalInput").ap()
    t["rope_sin_in"] = nc.dram_tensor("rope_sin_in", [128, c.Tc], F32, kind="ExternalInput").ap()
    t["mask_in"] = nc.dram_tensor("mask_in", [128, 512], F32, kind="ExternalInput").ap()
    t["ident_in"] = nc.dram_tensor("ident_in", [128, 128], F32, kind="ExternalInput").ap()
    t["y"] = nc.dram_tensor("y", [c.NT, 128, c.D], F32, kind="ExternalOutput").ap()
    ksz = c.H * 128 * c.Tc
    t["k_src_flat"] = nc.dram_tensor("k_src", [ksz], F32, kind="Internal").ap()
    t["v_src_flat"] = nc.dram_tensor("v_src", [ksz], F32, kind="Internal").ap()
    t["k_dst_flat"] = nc.dram_tensor("k_dst", [4 * ksz], F32, kind="Internal").ap()
    t["v_dst_flat"] = nc.dram_tensor("v_dst", [4 * ksz], F32, kind="Internal").ap()
    t["k_src"] = t["k_src_flat"].rearrange("(h p t) -> h p t", h=c.H, p=128)
    t["v_src"] = t["v_src_flat"].rearrange("(h t p) -> h t p", h=c.H, p=128)
    t["k_dst"] = t["k_dst_flat"].rearrange("(r h p t) -> r h p t", r=4, h=c.H, p=128)
    t["v_dst"] = t["v_dst_flat"].rearrange("(r h t p) -> r h t p", r=4, h=c.H, p=128)
    t["q_st"] = nc.dram_tensor("q_st", [c.H, 128, c.Tc], F32, kind="Internal").ap()
    t["o_st"] = nc.dram_tensor("o_st", [c.H, 128, c.Tc], F32, kind="Internal").ap()

    KT = c.KD * c.Tc
    szA = 4 * c.Tc + 3 * c.D
    szB = szA + 2 * c.IGC * 128 + c.IGC * c.Tc + c.Tc
    szC = 3 * c.S + 3 * c.Tc
    szD = 2 * c.KD * 128 + 4 * c.D
    assert c.IGC == 4
    HIC = c.IGC // 2
    HK = HIC * 128
    sizes = {"R1": max(KT, 4 * c.S, c.H * c.Tc),
             "R2": max(KT, 3 * c.KD * 128),
             "R3": max(szA, szB, szC, szD),
             "R4": 2 * c.Tc + 512 + 128 + 64,
             "R5": 2 * HK}
    total = sum(sizes.values())
    arena = nc.alloc_sbuf_tensor("arena", [128, total], F32).ap()
    arena_addr = None
    for al in nc.m.functions[0].allocations:
        if isinstance(al, mybir.MemoryLocationSet):
            for ml in al.memorylocations:
                if ml.name == "arena":
                    arena_addr = ml.addr
    assert arena_addr is not None
    off = {}
    o = 0
    for k, v in sizes.items():
        off[k] = o
        o += v

    def sl(base, start, ln):
        return arena[:, off[base] + start: off[base] + start + ln]

    def slr(name, base, start, ln):
        h = nc.alloc_sbuf_tensor_at(name, [128, ln], F32R,
                                    offset=arena_addr + 4 * (off[base] + start))
        return h.ap()

    t["xn_f"] = sl("R1", 0, KT)
    t["khbuf"] = [sl("R1", 0, c.S), sl("R1", c.S, c.S)]
    t["vhbuf"] = [sl("R1", 2 * c.S, c.S), sl("R1", 3 * c.S, c.S)]
    t["o_f"] = sl("R1", 0, c.H * c.Tc)
    t["down_acc"] = sl("R2", 0, KT)
    t["wbufA"] = [sl("R2", 0, c.KD * 128), sl("R2", c.KD * 128, c.KD * 128)]
    t["ubuf1tmp"] = sl("R2", 2 * c.KD * 128, c.KD * 128)
    t["qkvbuf"] = sl("R3", 0, c.Tc)
    t["ropetmp"] = sl("R3", c.Tc, c.Tc)
    t["ropebuf"] = [sl("R3", 2 * c.Tc, c.Tc), sl("R3", 3 * c.Tc, c.Tc)]
    t["vtbuf"] = t["ropebuf"]
    t["xbuf"] = [sl("R3", 4 * c.Tc, c.D), sl("R3", 4 * c.Tc + c.D, c.D)]
    t["xsq2"] = sl("R3", 4 * c.Tc + 2 * c.D, c.D)
    t["xntok"] = t["xsq2"]
    t["gbuf"] = [sl("R3", 4 * c.Tc, c.D), sl("R3", 4 * c.Tc + c.D, c.D)]
    t["ubuf"] = [sl("R3", 4 * c.Tc + 2 * c.D, c.D), sl("R3", 0, c.D)]
    b0 = szA
    t["dbuf"] = [sl("R3", b0, HK), sl("R3", b0 + HK, HK)]
    t["dblo"] = [slr("dblo0", "R3", b0, HK), slr("dblo1", "R3", b0 + HK, HK)]
    t["dbhi"] = [slr("dbhi0", "R5", 0, HK), slr("dbhi1", "R5", HK, HK)]
    t["h_g"] = sl("R3", b0 + 2 * c.IGC * 128, c.IGC * c.Tc)
    hg0 = b0 + 2 * c.IGC * 128
    t["hg_lo"] = [slr("hglo0", "R3", hg0, HIC * c.Tc),
                  slr("hglo1", "R3", hg0 + HIC * c.Tc, HIC * c.Tc)]
    t["hg_hi"] = [slr("hghi0", "R3", 2 * c.Tc, HIC * c.Tc),
                  slr("hghi1", "R4", 0, HIC * c.Tc)]
    t["silubuf"] = sl("R3", b0 + 2 * c.IGC * 128 + c.IGC * c.Tc, c.Tc)
    t["scores2"] = [sl("R3", 0, c.S), sl("R3", c.S, c.S)]
    t["pt"] = sl("R3", 2 * c.S, c.S)
    t["qhbuf"] = [sl("R3", 3 * c.S, c.Tc), sl("R3", 3 * c.S + c.Tc, c.Tc)]
    t["obuf"] = sl("R3", 3 * c.S + 2 * c.Tc, c.Tc)
    t["wobuf"] = [sl("R3", 0, c.KD * 128), sl("R3", c.KD * 128, c.KD * 128)]
    d0 = 2 * c.KD * 128
    t["xbuf2"] = [sl("R3", d0, c.D), sl("R3", d0 + c.D, c.D)]
    t["ybuf"] = [sl("R3", d0 + 2 * c.D, c.D), sl("R3", d0 + 3 * c.D, c.D)]
    t["rope_cos"] = sl("R4", 0, c.Tc)
    t["rope_sin"] = sl("R4", c.Tc, c.Tc)
    t["mask"] = sl("R4", 2 * c.Tc, 512)
    t["ident"] = sl("R4", 2 * c.Tc + 512, 128)
    v0 = 2 * c.Tc + 512 + 128
    for i, n in enumerate(["s1", "s2", "mvec", "mm", "var", "std", "rvec", "bvec",
                           "negmax0", "negmax1", "zvec0", "zvec1", "rz0", "rz1"]):
        t[n] = sl("R4", v0 + i * 2, 1)
    t["negmax2"] = [t["negmax0"], t["negmax1"]]
    t["zvec2"] = [t["zvec0"], t["zvec1"]]
    t["rz2"] = [t["rz0"], t["rz1"]]
    t["bmax2"] = [sl("R4", v0 + 28, c.NT), sl("R4", v0 + 28 + c.NT, c.NT)]

    with ExitStack() as stx:
        psum = [stx.enter_context(nc.psum_tensor(f"ps{i}", [128, 512], F32))
                for i in range(8)]
        t["psum"] = psum
        semnames = ["sc", "sx0", "sx1", "swa0", "swa1", "sstq0", "sstq1",
                    "sstk0", "sstk1", "sstv0", "sstv1", "sgu0", "sgu1",
                    "sd0", "sd1", "sccK", "sccV", "skh0", "skh1", "svh0",
                    "svh1", "sqh0", "sqh1", "ssto", "sof", "sof2", "swo0",
                    "swo1", "sx20", "sx21", "so0", "so1", "PE", "VA", "VD"]
        sems = {n: Sem(n, stx.enter_context(nc.semaphore(n))) for n in semnames}

        marks = {}
        for en in ["sp", "pe", "act", "dve", "pool"]:
            for s in sems.values():
                s.n = 0
            b = Builder(cfg, t, sems, marks)
            b.run_all(en, None)
        def mk(en):
            def emit(eng):
                for s in sems.values():
                    s.n = 0
                b = Builder(cfg, t, sems, marks)
                b.final = True
                b.run_all(en, eng)
            return emit

        with nc.Block() as block:
            block.sync(mk("sp"))
            block.tensor(mk("pe"))
            block.scalar(mk("act"))
            block.vector(mk("dve"))
            block.gpsimd(mk("pool"))
    return nc


def _host_inputs(cfg: Cfg, x, wq, wk, wv, wo, wgu, wd):
    c = cfg
    xs = np.ascontiguousarray(np.asarray(x, dtype=np.float32))
    wqT = np.ascontiguousarray(np.asarray(wq, dtype=np.float32).T)
    wkT = np.ascontiguousarray(np.asarray(wk, dtype=np.float32).T)
    wvT = np.ascontiguousarray(np.asarray(wv, dtype=np.float32).T)
    woT = np.ascontiguousarray(np.asarray(wo, dtype=np.float32).T)
    wguT = np.ascontiguousarray(np.asarray(wgu, dtype=np.float32).T)
    wdT = np.ascontiguousarray(np.asarray(wd, dtype=np.float32).T)
    ident = np.eye(128, dtype=np.float32)
    c128 = 1.0 / (128.0 ** 0.25)
    inv_freq = 1.0 / (10000.0 ** (np.arange(0, 128, 2, dtype=np.float64) / 128.0))
    in_maps = []
    for core in range(8):
        bb, r = core // 4, core % 4
        tiles = [4 * j + r for j in range(c.NT)]
        x_own = np.stack([xs[bb, 128 * g:128 * (g + 1), :] for g in tiles])
        pos = np.concatenate([128 * g + np.arange(128) for g in tiles]).astype(np.float64)
        ang = pos[None, :] * inv_freq[:, None]
        cos = np.cos(ang)
        sin = np.sin(ang)
        rope_cos = np.empty((128, c.Tc), dtype=np.float32)
        rope_sin = np.empty((128, c.Tc), dtype=np.float32)
        rope_cos[0::2] = cos * c128
        rope_cos[1::2] = cos * c128
        rope_sin[0::2] = -sin * c128
        rope_sin[1::2] = sin * c128
        mask = np.full((128, 512), NEG_BIG, dtype=np.float32)
        for i in range(128):
            mask[i, :128 * r + i + 1] = 0.0
        in_maps.append({
            "x_own": x_own, "wqT": wqT, "wkT": wkT, "wvT": wvT, "woT": woT,
            "wguT": wguT, "wdT": wdT, "rope_cos_in": rope_cos,
            "rope_sin_in": rope_sin, "mask_in": mask, "ident_in": ident,
        })
    return in_maps


def _assemble(cfg: Cfg, results):
    c = cfg
    out = np.empty((2, c.S, c.D), dtype=np.float32)
    for core in range(8):
        bb, r = core // 4, core % 4
        y = np.asarray(results[core]["y"])
        for j in range(c.NT):
            g = 4 * j + r
            out[bb, 128 * g:128 * (g + 1), :] = y[j]
    return out


def kernel(L0_attn_x, L0_attn_Wq, L0_attn_Wk, L0_attn_Wv, L0_attn_Wo,
           L0_ffn_w_gate_up, L0_ffn_w_down):
    cfg = Cfg()
    in_maps = _host_inputs(cfg, L0_attn_x, L0_attn_Wq, L0_attn_Wk, L0_attn_Wv,
                           L0_attn_Wo, L0_ffn_w_gate_up, L0_ffn_w_down)
    nc = build_nc(cfg)
    res = run_bass_kernel_spmd(nc, in_maps, core_ids=list(range(8)))
    return _assemble(cfg, res.results)


# version 18 (final)
# speedup vs baseline: 1.1375x; 1.0519x over previous
import numpy as np
from contextlib import ExitStack

import concourse.bass as bass
import concourse.mybir as mybir
from concourse.bass_utils import run_bass_kernel_spmd

F32 = mybir.dt.float32
F32R = mybir.dt.float32r
AF = mybir.ActivationFunctionType
ALU = mybir.AluOpType
AXX = mybir.AxisListType.X

NEG_BIG = -1.0e30
LN_EPS = 1e-5


class Cfg:
    def __init__(self, D=2048, H=16, S=4096, INTER=8192, NG=16):
        self.D, self.H, self.S, self.I = D, H, S, INTER
        assert D == H * 128
        self.KD = D // 128
        self.Tc = S // 4
        self.NT = self.Tc // 128
        assert self.NT == S // 512
        self.TM = min(512, self.Tc)
        self.NTM = self.Tc // self.TM
        self.NKT = S // 128
        self.NG = NG
        self.IG = INTER // NG
        self.IGC = self.IG // 128
        assert self.IG * NG == INTER and self.IGC * 128 == self.IG


class Px:

    def __init__(self, eng):
        self._e = eng

    @property
    def live(self):
        return self._e is not None

    def __getattr__(self, name):
        if self._e is None:
            return lambda *a, **k: None
        return getattr(self._e, name)


class Sem:
    def __init__(self, name, handle=None):
        self.name, self.h, self.n = name, handle, 0


class Builder:

    def __init__(self, cfg, tensors, sems, marks):
        self.c = cfg
        self.t = tensors
        self.sems = sems
        self.marks = marks
        self.banks = {}
        self.final = False
        self.hiwater = {}

    def inc(self, inst, semname, k, key=None):
        s = self.sems[semname]
        s.n += k
        if inst is not None:
            inst.then_inc(s.h, k)
        if key is not None:
            if not self.final:
                self.marks[key] = (semname, s.n)
            else:
                assert self.marks[key] == (semname, s.n), (key, self.marks[key], (semname, s.n))
        return key

    def mark_here(self, semname, key):
        if not self.final:
            self.marks[key] = (semname, self.sems[semname].n)
        else:
            assert self.marks[key] == (semname, self.sems[semname].n), key
        return key

    def wait(self, eng, key):
        if key is None:
            return
        if key not in self.marks:
            neg = any(isinstance(x, int) and x < 0 for x in key)
            assert neg or not self.final, f"missing mark {key}"
            return
        semname, v = self.marks[key]
        if v <= self.hiwater.get(semname, 0):
            return
        self.hiwater[semname] = v
        if eng.live:
            eng.wait_ge(self.sems[semname].h, v)

    def bank_use(self, en, eng, bank, evac_key):
        prev = self.banks.get(bank)
        if en == "pe" and prev is not None:
            self.wait(eng, prev)
        self.banks[bank] = evac_key

    def phaseA(self, en, e):
        c, t = self.c, self.t
        atp = [0]
        if en == "sp":
            for nm_in, nm in [("rope_cos_in", "rope_cos"), ("rope_sin_in", "rope_sin"),
                              ("mask_in", "mask"), ("ident_in", "ident")]:
                inst = e.dma_start(out=t[nm], in_=t[nm_in])
                self.inc(inst, "sc", 16)
            self.mark_here("sc", ("consts",))
        for j in range(c.NT):
            xb = t["xbuf"][j % 2]
            if en == "sp":
                self.wait(e, ("xbuf_free", j - 2))
                self.wait(e, ("rb", j - 2))
                self.inc(e.dma_start(out=xb, in_=t["x_own"][j]), f"sx{j % 2}", 16,
                         ("xld", j))
            if en == "dve":
                self.wait(e, ("xld", j))
                self.inc(e.reduce_sum(t["s1"], xb, axis=AXX), "VD", 1, ("ls1", j))
                self.wait(e, ("sq", j))
                self.wait(e, ("ls1", j))
                self.inc(e.tensor_scalar_mul(t["mvec"], t["s1"], 1.0 / c.D),
                         "VD", 1, ("lmv", j))
                self.wait(e, ("lmv", j))
                self.inc(e.tensor_mul(t["mm"], t["mvec"], t["mvec"]), "VD", 1, ("lmm", j))
                self.wait(e, ("lmm", j))
                self.inc(e.scalar_tensor_tensor(t["var"], t["s2"], 1.0 / c.D, t["mm"],
                                                ALU.mult, ALU.subtract),
                         "VD", 1, ("lv0", j))
                self.wait(e, ("lv0", j))
                inst = e.tensor_scalar_add(t["var"], t["var"], LN_EPS)
                self.inc(inst, "VD", 1, ("var", j))
                self.wait(e, ("std", j))
                self.inc(e.reciprocal(t["rvec"], t["std"]), "VD", 1, ("lrv", j))
                self.wait(e, ("lrv", j))
                inst = e.scalar_tensor_tensor(t["bvec"], t["mvec"], -1.0, t["rvec"],
                                              ALU.mult, ALU.mult)
                self.inc(inst, "VD", 1, ("rb", j))
            if en == "act":
                self.wait(e, ("xld", j))
                self.wait(e, ("tpA", j - 1, c.KD - 1))
                inst = e.activation(t["xsq2"], xb, AF.Square, accum_out=t["s2"])
                self.inc(inst, "VA", 1, ("sq", j))
                self.wait(e, ("var", j))
                inst = e.activation(t["std"], t["var"], AF.Sqrt)
                self.inc(inst, "VA", 1, ("std", j))
                self.wait(e, ("rb", j))
                inst = e.activation(t["xntok"], xb, AF.Identity,
                                    bias=t["bvec"], scale=t["rvec"])
                self.inc(inst, "VA", 1, ("xn", j))
                self.mark_here("VA", ("xbuf_free", j))
            for kd in range(c.KD):
                bank = 4 + (atp[0] % 4)
                ev = "dve" if atp[0] % 2 == 0 else "act"
                atp[0] += 1
                self.bank_use(en, e, bank, ("tpAe", j, kd))
                if en == "pe":
                    if kd == 0:
                        if j == 0:
                            self.wait(e, ("consts",))
                        self.wait(e, ("xn", j))
                    inst = e.transpose(t["psum"][bank][:, 0:128],
                                       t["xntok"][:, kd * 128:(kd + 1) * 128],
                                       t["ident"])
                    self.inc(inst, "PE", 1, ("tpA", j, kd))
                if en == ev:
                    self.wait(e, ("tpA", j, kd))
                    dst = t["xn_f"][:, kd * c.Tc + j * 128: kd * c.Tc + (j + 1) * 128]
                    if ev == "dve":
                        inst = e.tensor_copy(dst, t["psum"][bank][:, 0:128])
                        self.inc(inst, "VD", 1, ("tpAe", j, kd))
                    else:
                        inst = e.activation(dst, t["psum"][bank][:, 0:128], AF.Copy)
                        self.inc(inst, "VA", 1, ("tpAe", j, kd))
            if ((j + 1) * 128) % c.TM == 0:
                h2 = ((j + 1) * 128) // c.TM - 1
                if en == "dve":
                    self.mark_here("VD", ("xnf_half", h2))
                if en == "act":
                    self.mark_here("VA", ("xnf_halfA", h2))
        if en == "dve":
            self.mark_here("VD", ("xnf_done",))
        if en == "act":
            self.mark_here("VA", ("xnf_doneA",))

        for wi, wname in enumerate(["wqT", "wkT", "wvT"]):
            for mc in range(c.H):
                it = wi * c.H + mc
                wb = t["wbufA"][it % 2]
                if en == "sp":
                    self.wait(e, ("wA_free", it - 2))
                    src = t[wname][:, mc * 128:(mc + 1) * 128] \
                        .rearrange("(kd p) m -> p kd m", p=128)
                    dst = wb.rearrange("p (kd m) -> p kd m", m=128)
                    self.inc(e.dma_start(out=dst, in_=src), f"swa{it % 2}", 16,
                             ("wA", it))
                for tm in range(c.NTM):
                    bank = (it * c.NTM + tm) % 2
                    ps = t["psum"][bank][:, 0:c.TM]
                    self.bank_use(en, e, bank, ("pev", it, tm))
                    if en == "pe":
                        if tm == 0:
                            self.wait(e, ("wA", it))
                        if it == 0:
                            self.wait(e, ("xnf_half", tm))
                            self.wait(e, ("xnf_halfA", tm))
                        for kd in range(c.KD):
                            inst = e.matmul(
                                ps,
                                wb[:, kd * 128:(kd + 1) * 128],
                                t["xn_f"][:, kd * c.Tc + tm * c.TM:
                                          kd * c.Tc + (tm + 1) * c.TM],
                                start=(kd == 0), stop=(kd == c.KD - 1))
                            self.inc(inst, "PE", 1,
                                     ("proj", it, tm) if kd == c.KD - 1 else None)
                        if tm == c.NTM - 1:
                            self.mark_here("PE", ("wA_free", it))
                    if en == "act":
                        if tm == 0:
                            self.wait(e, ("qkvbuf_free", it - 1))
                        self.wait(e, ("proj", it, tm))
                        dst = t["qkvbuf"][:, tm * c.TM:(tm + 1) * c.TM]
                        inst = e.activation(dst, ps, AF.Copy)
                        self.inc(inst, "VA", 1, ("pev", it, tm))
                if wname in ("wqT", "wkT"):
                    if en == "dve":
                        if it == 0:
                            self.wait(e, ("consts",))
                        self.wait(e, ("pev", it, c.NTM - 1))
                        self.wait(e, ("abuf_free", it - 2))
                        self.wait(e, ("rope", it - 1))
                        self.inc(e.stream_shuffle(t["ropetmp"], t["qkvbuf"],
                                                  mask=[i ^ 1 for i in range(32)]),
                                 "VD", 1, ("rsh", it))
                        self.inc(e.tensor_mul(t["ropebuf"][it % 2], t["qkvbuf"],
                                              t["rope_cos"]), "VD", 1, ("rm1", it))
                        self.wait(e, ("rsh", it))
                        self.inc(e.tensor_mul(t["ropetmp"], t["ropetmp"],
                                              t["rope_sin"]), "VD", 1, ("rm2", it))
                        self.wait(e, ("rm2", it))
                        inst = e.tensor_add(t["ropebuf"][it % 2],
                                            t["ropebuf"][it % 2], t["ropetmp"])
                        self.inc(inst, "VD", 1, ("rope", it))
                        self.mark_here("VD", ("qkvbuf_free", it))
                    if en == "pool":
                        self.wait(e, ("rope", it))
                        dst = (t["q_st"] if wname == "wqT" else t["k_src"])[mc]
                        inst = e.dma_start(out=dst, in_=t["ropebuf"][it % 2])
                        sn = f"sstq{it % 2}" if wname == "wqT" else f"sstk{it % 2}"
                        self.inc(inst, sn, 16, ("stQK", it))
                        self.mark_here(sn, ("abuf_free", it))
                else:
                    for j in range(c.NT):
                        bank = 4 + (atp[0] % 4)
                        ev = "dve" if atp[0] % 2 == 0 else "act"
                        atp[0] += 1
                        self.bank_use(en, e, bank, ("tpVe", it, j))
                        if en == "pe":
                            if j == 0:
                                self.wait(e, ("pev", it, c.NTM - 1))
                            inst = e.transpose(
                                t["psum"][bank][:, 0:128],
                                t["qkvbuf"][:, j * 128:(j + 1) * 128],
                                t["ident"])
                            self.inc(inst, "PE", 1, ("tpV", it, j))
                            if j == c.NT - 1:
                                self.mark_here("PE", ("qkvbuf_free", it))
                                if it == 3 * c.H - 1:
                                    self.mark_here("PE", ("Adone_pe",))
                        if en == ev:
                            if j < 2:
                                self.wait(e, ("abuf_free", it - 2))
                            self.wait(e, ("tpV", it, j))
                            dst = t["vtbuf"][it % 2][:, j * 128:(j + 1) * 128]
                            if ev == "dve":
                                inst = e.tensor_copy(dst, t["psum"][bank][:, 0:128])
                                self.inc(inst, "VD", 1, ("tpVe", it, j))
                            else:
                                inst = e.activation(dst, t["psum"][bank][:, 0:128],
                                                    AF.Copy)
                                self.inc(inst, "VA", 1, ("tpVe", it, j))
                    if en == "pool":
                        self.wait(e, ("tpVe", it, c.NT - 1))
                        self.wait(e, ("tpVe", it, c.NT - 2))
                        dst = t["v_src"][mc].rearrange("(j p) h -> p j h", p=128)
                        src = t["vtbuf"][it % 2].rearrange("p (j h) -> p j h", h=128)
                        inst = e.dma_start(out=dst, in_=src)
                        self.inc(inst, f"sstv{it % 2}", 16, ("stV", it))
                        self.mark_here(f"sstv{it % 2}", ("abuf_free", it))

        if en == "pool":
            if e.live:
                for sn in ["sstk0", "sstk1", "sstv0", "sstv1"]:
                    e.wait_ge(self.sems[sn].h, self.sems[sn].n)
            inst = e.collective_compute(
                "AllGather", ALU.bypass,
                replica_groups=[[0, 1, 2, 3], [4, 5, 6, 7]],
                ins=[t["k_src_flat"]], outs=[t["k_dst_flat"]])
            self.inc(inst, "sccK", 1, ("ccK",))
            inst = e.collective_compute(
                "AllGather", ALU.bypass,
                replica_groups=[[0, 1, 2, 3], [4, 5, 6, 7]],
                ins=[t["v_src_flat"]], outs=[t["v_dst_flat"]])
            self.inc(inst, "sccV", 1, ("ccV",))

    def phaseB(self, en, e):
        c, t = self.c, self.t
        HIC = c.IGC // 2
        HK = HIC * 128
        for g in range(c.NG):
            for ic in range(c.IGC):
                it = g * c.IGC + ic
                gb = t["gbuf"][it % 2]
                ub = t["ubuf1tmp"] if it == 1 else t["ubuf"][it % 2]
                if en == "sp":
                    if it == 0:
                        self.wait(e, ("tpA", c.NT - 1, c.KD - 1))
                        self.wait(e, ("rb", c.NT - 1))
                        self.wait(e, ("xn", c.NT - 1))
                    if it == 3:
                        self.wait(e, ("qkvbuf_free", 3 * c.H - 1))
                        self.wait(e, ("rope", 2 * c.H - 1))
                    self.wait(e, ("gu_free", it - 2))
                    col = g * c.IG + ic * 128
                    src = t["wguT"][:, col:col + 128].rearrange("(kd p) m -> p kd m", p=128)
                    self.inc(e.dma_start(out=gb.rearrange("p (kd m) -> p kd m", m=128),
                                         in_=src), f"sgu{it % 2}", 16, None)
                    srcu = t["wguT"][:, c.I + col:c.I + col + 128] \
                        .rearrange("(kd p) m -> p kd m", p=128)
                    self.inc(e.dma_start(out=ub.rearrange("p (kd m) -> p kd m", m=128),
                                         in_=srcu), f"sgu{it % 2}", 16, ("guld", it))
                for tm in range(c.NTM):
                    L = it * c.NTM + tm
                    bg = (2 * L) % 4
                    bu = bg + 1
                    psg = t["psum"][bg][:, 0:c.TM]
                    psu = t["psum"][bu][:, 0:c.TM]
                    self.bank_use(en, e, bg, ("silu", it, tm))
                    self.bank_use(en, e, bu, ("hslab", it, tm))
                    if en == "pe":
                        if tm == 0:
                            self.wait(e, ("guld", it))
                            if it == 0:
                                self.wait(e, ("xnf_done",))
                                self.wait(e, ("xnf_doneA",))
                        for kd in range(c.KD):
                            rhs = t["xn_f"][:, kd * c.Tc + tm * c.TM:
                                            kd * c.Tc + (tm + 1) * c.TM]
                            inst = e.matmul(psg, gb[:, kd * 128:(kd + 1) * 128], rhs,
                                            start=(kd == 0), stop=(kd == c.KD - 1))
                            self.inc(inst, "PE", 1, None)
                        for kd in range(c.KD):
                            rhs = t["xn_f"][:, kd * c.Tc + tm * c.TM:
                                            kd * c.Tc + (tm + 1) * c.TM]
                            inst = e.matmul(psu, ub[:, kd * 128:(kd + 1) * 128], rhs,
                                            start=(kd == 0), stop=(kd == c.KD - 1))
                            self.inc(inst, "PE", 1,
                                     ("gu", it, tm) if kd == c.KD - 1 else None)
                        if tm == c.NTM - 1:
                            self.mark_here("PE", ("gu_free", it))
                        if it == c.NG * c.IGC - 1 and tm == c.NTM - 1:
                            self.mark_here("PE", ("gu_alldone",))
                    if en == "act":
                        self.wait(e, ("gu", it, tm))
                        self.wait(e, ("hslab_prev", it - 1, tm))
                        dst = t["silubuf"][:, tm * c.TM:(tm + 1) * c.TM]
                        inst = e.activation(dst, psg, AF.Silu)
                        self.inc(inst, "VA", 1, ("silu", it, tm))
                    if en == "dve":
                        self.wait(e, ("silu", it, tm))
                        self.wait(e, ("downdone", g - 1))
                        dst = t["h_g"][:, ic * c.Tc + tm * c.TM:
                                       ic * c.Tc + (tm + 1) * c.TM]
                        inst = e.tensor_mul(dst, psu,
                                            t["silubuf"][:, tm * c.TM:(tm + 1) * c.TM])
                        self.inc(inst, "VD", 1, ("hslab", it, tm))
                        self.mark_here("VD", ("hslab_prev", it, tm))
                        if tm == c.NTM - 1 and ic % HIC == HIC - 1:
                            hf = ic // HIC
                            self.wait(e, ("hslab", it, tm))
                            if g == 0:
                                if hf == 0:
                                    self.wait(e, ("abuf_free", 3 * c.H - 2))
                                    self.wait(e, ("abuf_free", 3 * c.H - 1))
                                else:
                                    self.wait(e, ("rope", 2 * c.H - 1))
                            hcol = hf * HIC * c.Tc
                            src = t["h_g"][:, hcol:hcol + HIC * c.Tc]
                            inst = e.tensor_copy(t["hg_hi"][hf], src)
                            self.inc(inst, "VD", 1, ("hghi", g, hf))
                            self.wait(e, ("hghi", g, hf))
                            inst = e.tensor_sub(t["hg_lo"][hf], src,
                                                t["hg_hi"][hf].bitcast(F32))
                            self.inc(inst, "VD", 1, ("hglo", g, hf))

            def cv(j):
                self.wait(e, ("dld", j))
                self.wait(e, ("d_free", j - 2))
                inst = e.tensor_copy(t["dbhi"][j % 2], t["dbuf"][j % 2])
                self.inc(inst, "VD", 1, ("dbhi", j))
                self.wait(e, ("dbhi", j))
                inst = e.tensor_sub(t["dblo"][j % 2], t["dbuf"][j % 2],
                                    t["dbhi"][j % 2].bitcast(F32))
                self.inc(inst, "VD", 1, ("dblo", j))

            for m2 in range(c.KD):
                it2 = g * c.KD + m2
                db = t["dbuf"][it2 % 2]
                dbh = t["dbhi"][it2 % 2]
                dbl = t["dblo"][it2 % 2]
                if en == "sp":
                    self.wait(e, ("d_free", it2 - 2))
                    src = t["wdT"][g * c.IG:(g + 1) * c.IG, m2 * 128:(m2 + 1) * 128] \
                        .rearrange("(kc p) m -> p kc m", p=128)
                    self.inc(e.dma_start(out=db.rearrange("p (kc m) -> p kc m", m=128),
                                         in_=src), f"sd{it2 % 2}", 16, ("dld", it2))
                if en == "dve" and m2 == 0:
                    cv(it2)
                    cv(it2 + 1)
                for tm in range(c.NTM):
                    bank = 4 + (it2 * c.NTM + tm) % 2
                    ps = t["psum"][bank][:, 0:c.TM]
                    self.bank_use(en, e, bank, ("dacc", it2, tm))
                    if en == "pe":
                        if tm == 0:
                            self.wait(e, ("dblo", it2))
                        if m2 == 0 and tm == 0:
                            self.wait(e, ("hglo", g, 0))
                            self.wait(e, ("hglo", g, 1))
                        i = 0
                        for hf in range(2):
                            for kc2 in range(HIC):
                                kc = hf * HIC + kc2
                                wh = dbh[:, kc * 128:(kc + 1) * 128]
                                wl = dbl[:, kc * 128:(kc + 1) * 128]
                                xcol = kc2 * c.Tc + tm * c.TM
                                xh = t["hg_hi"][hf][:, xcol:xcol + c.TM]
                                xl = t["hg_lo"][hf][:, xcol:xcol + c.TM]
                                for lh, rh in ((wh, xh), (wh, xl), (wl, xh)):
                                    inst = e.matmul(ps, lh, rh, start=(i == 0),
                                                    stop=(i == 6 * HIC - 1))
                                    i += 1
                                    self.inc(inst, "PE", 1,
                                             ("down", it2, tm) if i == 6 * HIC else None)
                        if m2 == c.KD - 1 and tm == c.NTM - 1:
                            self.mark_here("PE", ("downdone", g))
                        if tm == c.NTM - 1:
                            self.mark_here("PE", ("d_free", it2))
                    if en == "dve":
                        self.wait(e, ("down", it2, tm))
                        self.wait(e, ("dacc", (g - 1) * c.KD + m2, tm))
                        dst = t["down_acc"][:, m2 * c.Tc + tm * c.TM:
                                            m2 * c.Tc + (tm + 1) * c.TM]
                        if g == 0:
                            inst = e.tensor_copy(dst, ps)
                        else:
                            inst = e.tensor_add(dst, dst, ps)
                        self.inc(inst, "VD", 1, ("dacc", it2, tm))
                if en == "dve" and m2 + 2 < c.KD:
                    cv(it2 + 2)
        if en == "act":
            self.wait(e, ("dacc", c.NG * c.KD - 1, c.NTM - 1))
            inst = e.activation(t["down_acc"], t["down_acc"], AF.Silu)
            self.inc(inst, "VA", 1, ("gdone",))

    def phaseC(self, en, e):
        c, t = self.c, self.t
        T = c.H * c.NT
        tpctr = 0

        def S0(idx):
            h, jj = divmod(idx, c.NT)
            j = c.NT - 1 - jj
            kh, vh, qh = t["khbuf"][h % 2], t["vhbuf"][h % 2], t["qhbuf"][h % 2]
            nb = j + 1
            sc = t["scores2"][idx % 2]
            if jj == 0 and en == "pool":
                if h == 0:
                    self.wait(e, ("ccK",))
                    self.wait(e, ("ccV",))
                    if e.live:
                        for sn in ["sstq0", "sstq1"]:
                            e.wait_ge(self.sems[sn].h, self.sems[sn].n)
                if h < 2:
                    self.wait(e, ("gu", c.NG * c.IGC - 1, c.NTM - 1))
                self.wait(e, ("head_done", h - 2))
                if h < 2:
                    self.wait(e, ("downdone", c.NG - 1))
                    self.wait(e, ("hslab", c.NG * c.IGC - 1, c.NTM - 1))
                self.inc(e.dma_start(out=qh, in_=t["q_st"][h]), f"sqh{h % 2}", 16,
                         ("hld", h))
                khr = kh.rearrange("p (j r t) -> p j r t", r=4, t=128)
                vhr = vh.rearrange("p (j r h2) -> p j r h2", r=4, h2=128)
                for r in range(4):
                    src = t["k_dst"][r, h].rearrange("p (j t) -> p j t", t=128)
                    self.inc(e.dma_start(out=khr[:, :, r], in_=src), f"skh{h % 2}", 16,
                             ("khld", h) if r == 3 else None)
                    srcv = t["v_dst"][r, h].rearrange("(j p) h2 -> p j h2", p=128)
                    self.inc(e.dma_start(out=vhr[:, :, r], in_=srcv), f"svh{h % 2}", 16,
                             ("vhld", h) if r == 3 else None)
            for b in range(nb):
                bank = [0, 1, 2 + ((idx + 1) % 2)][b % 3]
                ps = t["psum"][bank][:, 0:512]
                self.bank_use(en, e, bank, ("Cse", idx, b))
                if en == "pe":
                    if b == 0 and jj == 0:
                        self.wait(e, ("hld", h))
                        self.wait(e, ("khld", h))
                        self.wait(e, ("vhld", h))
                    inst = e.matmul(ps, qh[:, j * 128:(j + 1) * 128],
                                    kh[:, b * 512:(b + 1) * 512],
                                    start=True, stop=True)
                    self.inc(inst, "PE", 1, ("Cs", idx, b))
                ev = "act" if b < j else "dve"
                if en == ev:
                    if idx < 2 and b in (0, j):
                        self.wait(e, ("abuf_free", 3 * c.H - 2))
                        self.wait(e, ("abuf_free", 3 * c.H - 1))
                    if (b == 0 or b == j) and idx >= 2:
                        self.wait(e, ("scfree", idx))
                    self.wait(e, ("Cs", idx, b))
                    if b < j:
                        inst = e.activation(sc[:, b * 512:(b + 1) * 512], ps, AF.Copy)
                        self.inc(inst, "VA", 1, ("Cse", idx, b))
                    else:
                        if idx == 0:
                            self.wait(e, ("consts",))
                        inst = e.tensor_add(sc[:, b * 512:(b + 1) * 512],
                                            ps, t["mask"])
                        self.inc(inst, "VD", 1, ("Cse", idx, b))
                if en == "dve":
                    self.wait(e, ("Cse", idx, b))
                    if b == 0:
                        self.wait(e, ("Cmax", idx - 2))
                    inst = e.reduce_max(t["bmax2"][idx % 2][:, b:b + 1],
                                        sc[:, b * 512:(b + 1) * 512], axis=AXX)
                    self.inc(inst, "VD", 1, ("Cbm", idx, b))

        def S1max(idx):
            h, jj = divmod(idx, c.NT)
            j = c.NT - 1 - jj
            nb = j + 1
            if en == "dve":
                self.wait(e, ("Cbm", idx, nb - 1))
                self.wait(e, ("Cexp", idx - 2))
                inst = e.reduce_max(t["negmax2"][idx % 2],
                                    t["bmax2"][idx % 2][:, 0:nb],
                                    axis=AXX, negate=True)
                self.inc(inst, "VD", 1, ("Cmax", idx))

        def S1exp(idx):
            h, jj = divmod(idx, c.NT)
            j = c.NT - 1 - jj
            nb = j + 1
            sc = t["scores2"][idx % 2]
            if en == "act":
                self.wait(e, ("Cmax", idx))
                inst = e.activation(sc[:, 0:nb * 512], sc[:, 0:nb * 512], AF.Exp,
                                    bias=t["negmax2"][idx % 2],
                                    accum_out=t["zvec2"][idx % 2])
                self.inc(inst, "VA", 1, ("Cexp", idx))

        def S1norm(idx):
            h, jj = divmod(idx, c.NT)
            j = c.NT - 1 - jj
            nb = j + 1
            sc = t["scores2"][idx % 2]
            if en == "dve":
                self.wait(e, ("Cexp", idx))
                self.wait(e, ("Cnorm", idx - 2))
                self.inc(e.reciprocal(t["rz2"][idx % 2], t["zvec2"][idx % 2]),
                         "VD", 1, ("Crz", idx))
                self.wait(e, ("Crz", idx))
                inst = e.tensor_scalar_mul(sc[:, 0:nb * 512], sc[:, 0:nb * 512],
                                           t["rz2"][idx % 2])
                self.inc(inst, "VD", 1, ("Cnorm", idx))

        def S2(idx):
            nonlocal tpctr
            h, jj = divmod(idx, c.NT)
            j = c.NT - 1 - jj
            nkt = 4 * (j + 1)
            sc = t["scores2"][idx % 2]
            for kt in range(nkt):
                bank = 4 + (tpctr % 4)
                tpctr += 1
                self.bank_use(en, e, bank, ("Cpt", idx, kt))
                if en == "pe":
                    if kt == 0:
                        self.wait(e, ("Cnorm", idx))
                    inst = e.transpose(t["psum"][bank][:, 0:128],
                                       sc[:, kt * 128:(kt + 1) * 128], t["ident"])
                    self.inc(inst, "PE", 1, ("Ct", idx, kt))
                    if kt == nkt - 1:
                        self.mark_here("PE", ("scfree", idx + 2))
                ev = "act" if kt % 2 == 0 else "dve"
                if en == ev:
                    if kt in (0, 1) and idx > 0:
                        self.wait(e, ("ptfree", idx))
                    self.wait(e, ("Ct", idx, kt))
                    dst = t["pt"][:, kt * 128:(kt + 1) * 128]
                    if ev == "act":
                        inst = e.activation(dst, t["psum"][bank][:, 0:128], AF.Copy)
                        self.inc(inst, "VA", 1, ("Cpt", idx, kt))
                    else:
                        inst = e.tensor_copy(dst, t["psum"][bank][:, 0:128])
                        self.inc(inst, "VD", 1, ("Cpt", idx, kt))

        def S3(idx):
            h, jj = divmod(idx, c.NT)
            j = c.NT - 1 - jj
            nkt = 4 * (j + 1)
            vh = t["vhbuf"][h % 2]
            avbank = 2 + (idx % 2)
            psav = t["psum"][avbank][:, 0:128]
            self.bank_use(en, e, avbank, ("Co", idx))
            if en == "pe":
                self.wait(e, ("Cpt", idx, nkt - 2))
                self.wait(e, ("Cpt", idx, nkt - 1))
                for kt in range(nkt):
                    inst = e.matmul(psav, vh[:, kt * 128:(kt + 1) * 128],
                                    t["pt"][:, kt * 128:(kt + 1) * 128],
                                    start=(kt == 0), stop=(kt == nkt - 1))
                    self.inc(inst, "PE", 1,
                             ("Cav", idx) if kt == nkt - 1 else None)
                self.mark_here("PE", ("ptfree", idx + 1))
                if jj == c.NT - 1:
                    self.mark_here("PE", ("head_done", h))
                if idx == T - 1:
                    self.mark_here("PE", ("attn_done",))
            if en == "act":
                self.wait(e, ("Cav", idx))
                if jj == 0:
                    self.wait(e, ("ost", h - 1))
                    if idx == 0:
                        self.wait(e, ("hslab", c.NG * c.IGC - 1, c.NTM - 1))
                inst = e.activation(t["obuf"][:, j * 128:(j + 1) * 128], psav,
                                    AF.Copy)
                self.inc(inst, "VA", 1, ("Co", idx))
            if jj == c.NT - 1 and en == "pool":
                for jj in range(c.NT):
                    self.wait(e, ("Co", h * c.NT + jj))
                inst = e.dma_start(out=t["o_st"][h], in_=t["obuf"])
                self.inc(inst, "ssto", 16, ("ost", h))

        for idx in range(T + 1):
            if idx >= 1:
                S1exp(idx - 1)
                S1norm(idx - 1)
            if idx < T:
                S0(idx)
                S1max(idx)
            if idx >= 1:
                S2(idx - 1)
                S3(idx - 1)

    def phaseD(self, en, e):
        c, t = self.c, self.t
        if en == "pool":
            early = [h for h in range(c.H) if (h // 4) % 2 == 0]
            late = [h for h in range(c.H) if (h // 4) % 2 == 1]
            for h in early:
                self.wait(e, ("head_done", c.H - 2))
                self.wait(e, ("ost", h))
                dst = t["o_f"][:, h * c.Tc:(h + 1) * c.Tc]
                self.inc(e.dma_start(out=dst, in_=t["o_st"][h]), "sof", 16, None)
            self.mark_here("sof", ("oearly",))
            for h in late:
                self.wait(e, ("attn_done",))
                self.wait(e, ("ost", h))
                dst = t["o_f"][:, h * c.Tc:(h + 1) * c.Tc]
                self.inc(e.dma_start(out=dst, in_=t["o_st"][h]), "sof2", 16, None)
            self.mark_here("sof2", ("oallld",))
        for mc in range(c.KD):
            wb = t["wobuf"][mc % 2]
            if en == "sp":
                if mc < 2:
                    self.wait(e, ("scfree", c.H * c.NT))
                self.wait(e, ("wo_free", mc - 2))
                src = t["woT"][:, mc * 128:(mc + 1) * 128] \
                    .rearrange("(kd p) m -> p kd m", p=128)
                self.inc(e.dma_start(out=wb.rearrange("p (kd m) -> p kd m", m=128),
                                     in_=src), f"swo{mc % 2}", 16, ("wold", mc))
            for tm in range(c.NTM):
                bank = (mc * c.NTM + tm) % 2
                ps = t["psum"][bank][:, 0:c.TM]
                self.bank_use(en, e, bank, ("Dsg", mc, tm))
                if en == "pe":
                    hseq = ([h for h in range(c.H) if (h // 4) % 2 == 0]
                            + [h for h in range(c.H) if (h // 4) % 2 == 1])
                    if tm == 0:
                        self.wait(e, ("wold", mc))
                    if mc == 0 and tm == 0:
                        self.wait(e, ("oearly",))
                    for i, h in enumerate(hseq):
                        if i == len(hseq) - c.H // 2 and mc == 0 and tm == 0:
                            self.wait(e, ("oallld",))
                        inst = e.matmul(ps, wb[:, h * 128:(h + 1) * 128],
                                        t["o_f"][:, h * c.Tc + tm * c.TM:
                                                 h * c.Tc + (tm + 1) * c.TM],
                                        start=(i == 0), stop=(i == c.H - 1))
                        self.inc(inst, "PE", 1,
                                 ("Dwo", mc, tm) if i == c.H - 1 else None)
                    if tm == c.NTM - 1:
                        self.mark_here("PE", ("wo_free", mc))
                if en == "dve":
                    self.wait(e, ("Dwo", mc, tm))
                    if mc == 0 and tm == 0:
                        self.wait(e, ("gdone",))
                    sl_ = t["down_acc"][:, mc * c.Tc + tm * c.TM:
                                        mc * c.Tc + (tm + 1) * c.TM]
                    inst = e.tensor_mul(sl_, ps, sl_)
                    self.inc(inst, "VD", 1, ("Dsg", mc, tm))
        for j in range(c.NT):
            xb = t["xbuf2"][j % 2]
            if en == "sp":
                if j < 2:
                    self.wait(e, ("scfree", c.H * c.NT + 1))
                self.wait(e, ("x2free", j - 2))
                self.inc(e.dma_start(out=xb, in_=t["x_own"][j]), f"sx2{j % 2}", 16,
                         ("x2", j))
            for kd in range(c.KD):
                bank = 4 + ((j * c.KD + kd) % 4)
                ev = "act" if kd % 2 == 0 else "dve"
                self.bank_use(en, e, bank, ("tpDe", j, kd))
                if en == "pe":
                    self.wait(e, ("Dsg", kd, (j * 128) // c.TM))
                    inst = e.transpose(
                        t["psum"][bank][:, 0:128],
                        t["down_acc"][:, kd * c.Tc + j * 128:kd * c.Tc + (j + 1) * 128],
                        t["ident"])
                    self.inc(inst, "PE", 1, ("tpD", j, kd))
                if en == ev:
                    if j < 2 and kd < 2:
                        self.wait(e, ("ost", c.H - 1))
                        self.wait(e, ("attn_done",))
                    if kd < 2:
                        self.wait(e, ("ytile_free", j - 2))
                    self.wait(e, ("tpD", j, kd))
                    dst = t["ybuf"][j % 2][:, kd * 128:(kd + 1) * 128]
                    if ev == "act":
                        inst = e.activation(dst, t["psum"][bank][:, 0:128], AF.Copy)
                        self.inc(inst, "VA", 1, ("tpDe", j, kd))
                    else:
                        inst = e.tensor_copy(dst, t["psum"][bank][:, 0:128])
                        self.inc(inst, "VD", 1, ("tpDe", j, kd))
            if en == "dve":
                self.wait(e, ("x2", j))
                for kd in range(c.KD):
                    self.wait(e, ("tpDe", j, kd))
                inst = e.tensor_add(t["ybuf"][j % 2], t["ybuf"][j % 2], xb)
                self.inc(inst, "VD", 1, ("yt", j))
                self.mark_here("VD", ("x2free", j))
            if en == "pool":
                self.wait(e, ("yt", j))
                inst = e.dma_start(out=t["y"][j], in_=t["ybuf"][j % 2])
                self.inc(inst, f"so{j % 2}", 16, ("yst", j))
                self.mark_here(f"so{j % 2}", ("ytile_free", j))
        if en == "pool" and e.live:
            e.wait_ge(self.sems["so0"].h, self.sems["so0"].n)
            e.wait_ge(self.sems["so1"].h, self.sems["so1"].n)

    def run_all(self, en, eng):
        for ph in [self.phaseA, self.phaseB, self.phaseC, self.phaseD]:
            ph(en, Px(eng))


def build_nc(cfg: Cfg):
    c = cfg
    nc = bass.Bass("TRN2", target_bir_lowering=False, num_devices=8)
    t = {}
    t["x_own"] = nc.dram_tensor("x_own", [c.NT, 128, c.D], F32, kind="ExternalInput").ap()
    for n, shp in [("wqT", [c.D, c.D]), ("wkT", [c.D, c.D]), ("wvT", [c.D, c.D]),
                   ("woT", [c.D, c.D]), ("wguT", [c.D, 2 * c.I]), ("wdT", [c.I, c.D])]:
        t[n] = nc.dram_tensor(n, shp, F32, kind="ExternalInput").ap()
    t["rope_cos_in"] = nc.dram_tensor("rope_cos_in", [128, c.Tc], F32, kind="ExternalInput").ap()
    t["rope_sin_in"] = nc.dram_tensor("rope_sin_in", [128, c.Tc], F32, kind="ExternalInput").ap()
    t["mask_in"] = nc.dram_tensor("mask_in", [128, 512], F32, kind="ExternalInput").ap()
    t["ident_in"] = nc.dram_tensor("ident_in", [128, 128], F32, kind="ExternalInput").ap()
    t["y"] = nc.dram_tensor("y", [c.NT, 128, c.D], F32, kind="ExternalOutput").ap()
    ksz = c.H * 128 * c.Tc
    t["k_src_flat"] = nc.dram_tensor("k_src", [ksz], F32, kind="Internal").ap()
    t["v_src_flat"] = nc.dram_tensor("v_src", [ksz], F32, kind="Internal").ap()
    t["k_dst_flat"] = nc.dram_tensor("k_dst", [4 * ksz], F32, kind="Internal").ap()
    t["v_dst_flat"] = nc.dram_tensor("v_dst", [4 * ksz], F32, kind="Internal").ap()
    t["k_src"] = t["k_src_flat"].rearrange("(h p t) -> h p t", h=c.H, p=128)
    t["v_src"] = t["v_src_flat"].rearrange("(h t p) -> h t p", h=c.H, p=128)
    t["k_dst"] = t["k_dst_flat"].rearrange("(r h p t) -> r h p t", r=4, h=c.H, p=128)
    t["v_dst"] = t["v_dst_flat"].rearrange("(r h t p) -> r h t p", r=4, h=c.H, p=128)
    t["q_st"] = nc.dram_tensor("q_st", [c.H, 128, c.Tc], F32, kind="Internal").ap()
    t["o_st"] = nc.dram_tensor("o_st", [c.H, 128, c.Tc], F32, kind="Internal").ap()

    KT = c.KD * c.Tc
    szA = 4 * c.Tc + 3 * c.D
    szB = szA + 2 * c.IGC * 128 + c.IGC * c.Tc + c.Tc
    szC = 3 * c.S + 3 * c.Tc
    szD = 2 * c.KD * 128 + 4 * c.D
    assert c.IGC == 4
    HIC = c.IGC // 2
    HK = HIC * 128
    sizes = {"R1": max(KT, 4 * c.S, c.H * c.Tc),
             "R2": max(KT, 3 * c.KD * 128),
             "R3": max(szA, szB, szC, szD),
             "R4": 2 * c.Tc + 512 + 128 + 64,
             "R5": 2 * c.IGC * 128}
    total = sum(sizes.values())
    arena = nc.alloc_sbuf_tensor("arena", [128, total], F32).ap()
    arena_addr = None
    for al in nc.m.functions[0].allocations:
        if isinstance(al, mybir.MemoryLocationSet):
            for ml in al.memorylocations:
                if ml.name == "arena":
                    arena_addr = ml.addr
    assert arena_addr is not None
    off = {}
    o = 0
    for k, v in sizes.items():
        off[k] = o
        o += v

    def sl(base, start, ln):
        return arena[:, off[base] + start: off[base] + start + ln]

    def slr(name, base, start, ln):
        h = nc.alloc_sbuf_tensor_at(name, [128, ln], F32R,
                                    offset=arena_addr + 4 * (off[base] + start))
        return h.ap()

    t["xn_f"] = sl("R1", 0, KT)
    t["khbuf"] = [sl("R1", 0, c.S), sl("R1", c.S, c.S)]
    t["vhbuf"] = [sl("R1", 2 * c.S, c.S), sl("R1", 3 * c.S, c.S)]
    t["o_f"] = sl("R1", 0, c.H * c.Tc)
    t["down_acc"] = sl("R2", 0, KT)
    t["wbufA"] = [sl("R2", 0, c.KD * 128), sl("R2", c.KD * 128, c.KD * 128)]
    t["ubuf1tmp"] = sl("R2", 2 * c.KD * 128, c.KD * 128)
    t["qkvbuf"] = sl("R3", 0, c.Tc)
    t["ropetmp"] = sl("R3", c.Tc, c.Tc)
    t["ropebuf"] = [sl("R3", 2 * c.Tc, c.Tc), sl("R3", 3 * c.Tc, c.Tc)]
    t["vtbuf"] = t["ropebuf"]
    t["xbuf"] = [sl("R3", 4 * c.Tc, c.D), sl("R3", 4 * c.Tc + c.D, c.D)]
    t["xsq2"] = sl("R3", 4 * c.Tc + 2 * c.D, c.D)
    t["xntok"] = t["xsq2"]
    t["gbuf"] = [sl("R3", 4 * c.Tc, c.D), sl("R3", 4 * c.Tc + c.D, c.D)]
    t["ubuf"] = [sl("R3", 4 * c.Tc + 2 * c.D, c.D), sl("R3", 0, c.D)]
    b0 = szA
    KB2 = c.IGC * 128
    t["dbuf"] = [sl("R3", b0, KB2), sl("R3", b0 + KB2, KB2)]
    t["dblo"] = [slr("dblo0", "R3", b0, KB2), slr("dblo1", "R3", b0 + KB2, KB2)]
    t["dbhi"] = [slr("dbhi0", "R5", 0, KB2), slr("dbhi1", "R5", KB2, KB2)]
    t["h_g"] = sl("R3", b0 + 2 * c.IGC * 128, c.IGC * c.Tc)
    hg0 = b0 + 2 * c.IGC * 128
    t["hg_lo"] = [slr("hglo0", "R3", hg0, HIC * c.Tc),
                  slr("hglo1", "R3", hg0 + HIC * c.Tc, HIC * c.Tc)]
    t["hg_hi"] = [slr("hghi0", "R3", 2 * c.Tc, HIC * c.Tc),
                  slr("hghi1", "R4", 0, HIC * c.Tc)]
    t["silubuf"] = sl("R3", b0 + 2 * c.IGC * 128 + c.IGC * c.Tc, c.Tc)
    t["scores2"] = [sl("R3", 0, c.S), sl("R3", c.S, c.S)]
    t["pt"] = sl("R3", 2 * c.S, c.S)
    t["qhbuf"] = [sl("R3", 3 * c.S, c.Tc), sl("R3", 3 * c.S + c.Tc, c.Tc)]
    t["obuf"] = sl("R3", 3 * c.S + 2 * c.Tc, c.Tc)
    t["wobuf"] = [sl("R3", 0, c.KD * 128), sl("R3", c.KD * 128, c.KD * 128)]
    d0 = 2 * c.KD * 128
    t["xbuf2"] = [sl("R3", d0, c.D), sl("R3", d0 + c.D, c.D)]
    t["ybuf"] = [sl("R3", d0 + 2 * c.D, c.D), sl("R3", d0 + 3 * c.D, c.D)]
    t["rope_cos"] = sl("R4", 0, c.Tc)
    t["rope_sin"] = sl("R4", c.Tc, c.Tc)
    t["mask"] = sl("R4", 2 * c.Tc, 512)
    t["ident"] = sl("R4", 2 * c.Tc + 512, 128)
    v0 = 2 * c.Tc + 512 + 128
    for i, n in enumerate(["s1", "s2", "mvec", "mm", "var", "std", "rvec", "bvec",
                           "negmax0", "negmax1", "zvec0", "zvec1", "rz0", "rz1"]):
        t[n] = sl("R4", v0 + i * 2, 1)
    t["negmax2"] = [t["negmax0"], t["negmax1"]]
    t["zvec2"] = [t["zvec0"], t["zvec1"]]
    t["rz2"] = [t["rz0"], t["rz1"]]
    t["bmax2"] = [sl("R4", v0 + 28, c.NT), sl("R4", v0 + 28 + c.NT, c.NT)]

    with ExitStack() as stx:
        psum = [stx.enter_context(nc.psum_tensor(f"ps{i}", [128, 512], F32))
                for i in range(8)]
        t["psum"] = psum
        semnames = ["sc", "sx0", "sx1", "swa0", "swa1", "sstq0", "sstq1",
                    "sstk0", "sstk1", "sstv0", "sstv1", "sgu0", "sgu1",
                    "sd0", "sd1", "sccK", "sccV", "skh0", "skh1", "svh0",
                    "svh1", "sqh0", "sqh1", "ssto", "sof", "sof2", "swo0",
                    "swo1", "sx20", "sx21", "so0", "so1", "PE", "VA", "VD"]
        sems = {n: Sem(n, stx.enter_context(nc.semaphore(n))) for n in semnames}

        marks = {}
        for en in ["sp", "pe", "act", "dve", "pool"]:
            for s in sems.values():
                s.n = 0
            b = Builder(cfg, t, sems, marks)
            b.run_all(en, None)
        def mk(en):
            def emit(eng):
                for s in sems.values():
                    s.n = 0
                b = Builder(cfg, t, sems, marks)
                b.final = True
                b.run_all(en, eng)
            return emit

        with nc.Block() as block:
            block.sync(mk("sp"))
            block.tensor(mk("pe"))
            block.scalar(mk("act"))
            block.vector(mk("dve"))
            block.gpsimd(mk("pool"))
    return nc


def _host_inputs(cfg: Cfg, x, wq, wk, wv, wo, wgu, wd):
    c = cfg
    xs = np.ascontiguousarray(np.asarray(x, dtype=np.float32))
    wqT = np.ascontiguousarray(np.asarray(wq, dtype=np.float32).T)
    wkT = np.ascontiguousarray(np.asarray(wk, dtype=np.float32).T)
    wvT = np.ascontiguousarray(np.asarray(wv, dtype=np.float32).T)
    woT = np.ascontiguousarray(np.asarray(wo, dtype=np.float32).T)
    wguT = np.ascontiguousarray(np.asarray(wgu, dtype=np.float32).T)
    wdT = np.ascontiguousarray(np.asarray(wd, dtype=np.float32).T)
    ident = np.eye(128, dtype=np.float32)
    c128 = 1.0 / (128.0 ** 0.25)
    inv_freq = 1.0 / (10000.0 ** (np.arange(0, 128, 2, dtype=np.float64) / 128.0))
    in_maps = []
    for core in range(8):
        bb, r = core // 4, core % 4
        tiles = [4 * j + r for j in range(c.NT)]
        x_own = np.stack([xs[bb, 128 * g:128 * (g + 1), :] for g in tiles])
        pos = np.concatenate([128 * g + np.arange(128) for g in tiles]).astype(np.float64)
        ang = pos[None, :] * inv_freq[:, None]
        cos = np.cos(ang)
        sin = np.sin(ang)
        rope_cos = np.empty((128, c.Tc), dtype=np.float32)
        rope_sin = np.empty((128, c.Tc), dtype=np.float32)
        rope_cos[0::2] = cos * c128
        rope_cos[1::2] = cos * c128
        rope_sin[0::2] = -sin * c128
        rope_sin[1::2] = sin * c128
        mask = np.full((128, 512), NEG_BIG, dtype=np.float32)
        for i in range(128):
            mask[i, :128 * r + i + 1] = 0.0
        in_maps.append({
            "x_own": x_own, "wqT": wqT, "wkT": wkT, "wvT": wvT, "woT": woT,
            "wguT": wguT, "wdT": wdT, "rope_cos_in": rope_cos,
            "rope_sin_in": rope_sin, "mask_in": mask, "ident_in": ident,
        })
    return in_maps


def _assemble(cfg: Cfg, results):
    c = cfg
    out = np.empty((2, c.S, c.D), dtype=np.float32)
    for core in range(8):
        bb, r = core // 4, core % 4
        y = np.asarray(results[core]["y"])
        for j in range(c.NT):
            g = 4 * j + r
            out[bb, 128 * g:128 * (g + 1), :] = y[j]
    return out


def kernel(L0_attn_x, L0_attn_Wq, L0_attn_Wk, L0_attn_Wv, L0_attn_Wo,
           L0_ffn_w_gate_up, L0_ffn_w_down):
    cfg = Cfg()
    in_maps = _host_inputs(cfg, L0_attn_x, L0_attn_Wq, L0_attn_Wk, L0_attn_Wv,
                           L0_attn_Wo, L0_ffn_w_gate_up, L0_ffn_w_down)
    nc = build_nc(cfg)
    res = run_bass_kernel_spmd(nc, in_maps, core_ids=list(range(8)))
    return _assemble(cfg, res.results)
